# version 01
# baseline (speedup 1.0000x reference)
import numpy as np
from contextlib import ExitStack

import concourse.bass as bass
import concourse.mybir as mybir
from concourse.bass_utils import run_bass_kernel_spmd

F32 = mybir.dt.float32
BF16 = mybir.dt.bfloat16
U32 = mybir.dt.uint32
AF = mybir.ActivationFunctionType
ALU = mybir.AluOpType
AX = mybir.AxisListType.X

D = 1024
NT = 2048
NTILE = 16
H = 16
DH = 64
NCORES = 8
LN_EPS = 1e-5
GLA_EPS = 1e-6

ENGINES = ("sync", "tensor", "vector", "scalar", "gpsimd")


class Ctr:
    __slots__ = ("h", "n", "name")

    def __init__(self, h, name):
        self.h = h
        self.n = 0
        self.name = name


class Prog:
    def __init__(self):
        self.nc = bass.Bass("TRN2", target_bir_lowering=False, debug=False)
        self.stack = ExitStack()
        self._sems = {}

    def sem(self, name):
        if name not in self._sems:
            h = self.stack.enter_context(self.nc.semaphore(name))
            self._sems[name] = Ctr(h, name)
        return self._sems[name]

    def sbuf(self, name, shape, dtype):
        return self.stack.enter_context(self.nc.sbuf_tensor(name, shape, dtype))

    def psum(self, name, shape, dtype):
        return self.stack.enter_context(self.nc.psum_tensor(name, shape, dtype))


class Emit:

    def __init__(self, p):
        self.p = p
        self.pend = {e: [] for e in ENGINES}
        self.eng_ctr = {e: p.sem("c_" + e) for e in ENGINES if e != "sync"}
        self.dma_sems = {}
        self.last_w = {}
        self.readers = {}
        self.hw = {e: {} for e in ENGINES}

    def dma_sem(self, eng, key):
        k = (eng, key)
        if k not in self.dma_sems:
            self.dma_sems[k] = self.p.sem(f"d{len(self.dma_sems)}_{key[:12]}")
        return self.dma_sems[k]

    def op(self, eng, fn, reads=(), writes=(), dma=False, n_dma=1, self_inc=0,
           dkey=None):
        if dma or self_inc:
            if dkey is None:
                dkey = writes[0] if writes else (reads[0] if reads else "anon")
            ctr = self.dma_sem(eng, dkey)
        else:
            ctr = self.eng_ctr[eng]
        waits = {}

        def need(dep):
            if dep is None:
                return
            c, n = dep
            if n <= 0 or self.hw[eng].get(c, 0) >= n:
                return
            waits[c] = max(waits.get(c, 0), n)

        for b in reads:
            need(self.last_w.get(b))
        for b in writes:
            need(self.last_w.get(b))
            for c, n in self.readers.get(b, {}).items():
                need((c, n))

        for c, n in waits.items():
            self.hw[eng][c] = max(self.hw[eng].get(c, 0), n)
        wl = [(c.h, n) for c, n in waits.items()]

        if self_inc:
            ctr.n += self_inc
        else:
            ctr.n += 16 * n_dma if dma else 1
        my_cnt = ctr.n
        inc_k = 16 if (dma or self_inc) else 1

        def closure(e, fn=fn, wl=wl, ctr=ctr, self_inc=self_inc, inc_k=inc_k):
            for (sh, n) in wl:
                e.wait_ge(sh, n)
            last = fn(e)
            if self_inc:
                return
            assert last is not None, "op fn must return its last instruction(s)"
            if not isinstance(last, (list, tuple)):
                last = [last]
            if inc_k == 16:
                for ins in last:
                    ins.then_inc(ctr.h, inc_k)
            else:
                last[-1].then_inc(ctr.h, 1)

        self.pend[eng].append(closure)
        for b in reads:
            self.readers.setdefault(b, {})[ctr] = my_cnt
        for b in writes:
            self.last_w[b] = (ctr, my_cnt)
            self.readers[b] = {}
        return (ctr, my_cnt)

    def flush(self):
        with self.p.nc.Block() as block:
            for e in ENGINES:
                ops = self.pend[e]
                if not ops:
                    continue

                def run_all(eng, ops=ops):
                    for cl in ops:
                        cl(eng)
                getattr(block, e)(run_all)
            final = [(c.h, c.n) for c in
                     list(self.eng_ctr.values()) + list(self.dma_sems.values()) if c.n > 0]

            def tail(eng):
                for (sh, n) in final:
                    eng.wait_ge(sh, n)
            block.sync(tail)
        self.pend = {e: [] for e in ENGINES}


def make_const(e, g_eng, ap, key, kind, sq=128):
    if kind == "ones":
        e.op(g_eng, lambda g: g.memset(ap, 1.0), writes=(key,))
        return
    fill0, cmp = (0.0, ALU.not_equal) if kind == "ident" else (1.0, ALU.is_gt)
    if kind == "ident":
        e.op(g_eng, lambda g: g.memset(ap, 0.0), writes=(key,))
        e.op(g_eng, lambda g: g.affine_select(out=ap, in_=ap, compare_op=ALU.not_equal,
                                              fill=1.0, base=0, pattern=[[-1, sq]],
                                              channel_multiplier=1),
             reads=(key,), writes=(key,))
    else:
        e.op(g_eng, lambda g: g.memset(ap, 1.0), writes=(key,))
        e.op(g_eng, lambda g: g.affine_select(out=ap, in_=ap, compare_op=ALU.is_gt,
                                              fill=0.0, base=0, pattern=[[-1, sq]],
                                              channel_multiplier=1),
             reads=(key,), writes=(key,))


def emit_ln_tiles(e, p, x_d, u_fm, bufs, n_tiles=NTILE, stat_out=None, ukey="u",
                  src_key=None):
    SY, TE, VE, SC = "sync", "tensor", "vector", "scalar"
    xt, stat, ubf, psT, ident_bf = (bufs["xt"], bufs["stat"], bufs["ubf"],
                                    bufs["psT"], bufs["ident_bf"])
    for i in range(n_tiles):
        xb, sb, ub = xt[i % 2], stat[i % 2], ubf[i % 2]
        xk, sk, uk = f"xt{i % 2}", f"stat{i % 2}", f"ubf{i % 2}"
        e.op(SY, lambda s, i=i, xb=xb: s.dma_start(
            out=xb[:], in_=x_d[i * 128:(i + 1) * 128, :]), writes=(xk,), dma=True,
            reads=(() if src_key is None else tuple(src_key)))
        e.op(VE, lambda v, xb=xb, sb=sb: v.bn_stats(out=sb[:, 0:6], in_=xb[:, 0:512]),
             reads=(xk,), writes=(sk,))
        e.op(VE, lambda v, xb=xb, sb=sb: v.bn_stats(out=sb[:, 6:12], in_=xb[:, 512:1024]),
             reads=(xk,), writes=(sk,))
        e.op(VE, lambda v, sb=sb: v.bn_aggr(out=sb[:, 12:14], in_=sb[:, 0:12]),
             reads=(sk,), writes=(sk,))
        e.op(VE, lambda v, sb=sb: v.tensor_scalar_add(sb[:, 13:14], sb[:, 13:14],
                                                      float(LN_EPS)),
             reads=(sk,), writes=(sk,))
        e.op(SC, lambda s, sb=sb: s.activation(out=sb[:, 14:15], in_=sb[:, 13:14],
                                               func=AF.Ln),
             reads=(sk,), writes=(sk,))
        e.op(SC, lambda s, sb=sb: s.activation(out=sb[:, 15:16], in_=sb[:, 14:15],
                                               func=AF.Exp, scale=-0.5),
             reads=(sk,), writes=(sk,))
        e.op(VE, lambda v, xb=xb, sb=sb, ub=ub: v.tensor_scalar(
            out=ub[:], in0=xb[:], scalar1=sb[:, 12:13], scalar2=sb[:, 15:16],
            op0=ALU.subtract, op1=ALU.mult), reads=(xk, sk), writes=(uk,))
        if stat_out is not None:
            e.op(VE, lambda v, sb=sb, i=i: v.tensor_copy(
                out=stat_out[:, 2 * i:2 * i + 2],
                in_=sb[:].rearrange("p (a b) -> p a b", b=1)[:, (12, 15), :].squeeze(2)
                if False else sb[:, 12:16:3]),
                reads=(sk,), writes=("lnstat_" + ukey,))

        def fn_tr(t, ub=ub):
            last = None
            for dc in range(8):
                last = t.transpose(psT[:, dc * 128:(dc + 1) * 128],
                                   ub[:, dc * 128:(dc + 1) * 128], ident_bf[:])
            return last
        e.op(TE, fn_tr, reads=(uk, "identbf"), writes=("psT",))
        e.op(VE, lambda v, i=i: v.tensor_copy(
            out=u_fm[:, :, i * 128:(i + 1) * 128],
            in_=psT[:].rearrange("p (a b) -> p a b", b=128)),
            reads=("psT",), writes=("ufm_" + ukey,))


def emit_proj_fm(e, dst, dk, u_fm, wsel, wkeys, psB, elu_bufs=None,
                 n_tg=4, psb_key="psB", ukey="ufm_u"):
    TE, VE, SC = "tensor", "vector", "scalar"
    for oc in range(8):
        for tg in range(n_tg):
            def fn_mm(t, oc=oc, tg=tg):
                last = None
                for dc in range(8):
                    last = t.matmul(psB[:], wsel(dc, oc),
                                    u_fm[:, dc, tg * 512:(tg + 1) * 512],
                                    start=(dc == 0), stop=(dc == 7))
                return last
            e.op(TE, fn_mm, reads=wkeys + (ukey,), writes=(psb_key,))
            if elu_bufs is not None:
                elu_sc, elu_sc2 = elu_bufs
                e.op(VE, lambda v: v.tensor_scalar(out=elu_sc[:], in0=psB[:], scalar1=0.0,
                                                   scalar2=None, op0=ALU.min),
                     reads=(psb_key,), writes=("elusc",))
                e.op(SC, lambda s: s.activation(out=elu_sc2[:], in_=elu_sc[:], func=AF.Exp),
                     reads=("elusc",), writes=("elusc2",))
                e.op(VE, lambda v, oc=oc, tg=tg: v.scalar_tensor_tensor(
                    out=dst[:, oc, tg * 512:(tg + 1) * 512], in0=psB[:], scalar=0.0,
                    in1=elu_sc2[:], op0=ALU.max, op1=ALU.add),
                    reads=(psb_key, "elusc2"), writes=(dk,))
            else:
                e.op(VE, lambda v, oc=oc, tg=tg: v.tensor_copy(
                    out=dst[:, oc, tg * 512:(tg + 1) * 512], in_=psB[:]),
                    reads=(psb_key,), writes=(dk,))


def emit_v_tm(e, v_tm, vk, u_fm, udk, wsel, wkeys, psB, psb_key="psB"):
    TE, VE = "tensor", "vector"
    for ti in range(NTILE):
        for ng in range(2):
            def fn_mm(t, ti=ti, ng=ng):
                last = None
                for dc in range(8):
                    last = t.matmul(psB[:], u_fm[:, dc, ti * 128:(ti + 1) * 128],
                                    wsel(dc, ng), start=(dc == 0), stop=(dc == 7))
                return last
            e.op(TE, fn_mm, reads=wkeys + ("ufm_" + udk,), writes=(psb_key,))
            e.op(VE, lambda v, ti=ti, ng=ng: v.tensor_copy(
                out=v_tm[:, ti].rearrange("p (h x) -> p h x", x=65)[:, ng * 8:(ng + 1) * 8, 0:64],
                in_=psB[:].rearrange("p (h d) -> p h d", d=64)),
                reads=(psb_key,), writes=(vk,))


def build_l0a():
    p = Prog()
    nc = p.nc
    e = Emit(p)
    x_d = nc.dram_tensor("x", [NT, D], F32, kind="ExternalInput").ap()
    wq_d = nc.dram_tensor("Wq", [D, D], F32, kind="ExternalInput").ap()
    wk_d = nc.dram_tensor("Wk", [D, D], F32, kind="ExternalInput").ap()
    wv_d = nc.dram_tensor("Wv", [D, D], F32, kind="ExternalInput").ap()
    wa_d = nc.dram_tensor("Wa", [D, H], F32, kind="ExternalInput").ap()
    ba_d = nc.dram_tensor("ba", [H], F32, kind="ExternalInput").ap()

    send_d = nc.dram_tensor("S_end", [128, 520], F32, kind="ExternalOutput").ap()
    acore_d = nc.dram_tensor("A_core", [1, H], F32, kind="ExternalOutput").ap()
    chs_d = nc.dram_tensor("chunkS", [16, 128, 520], F32, kind="ExternalOutput").ap()
    cha_d = nc.dram_tensor("chunkA", [16, H], F32, kind="ExternalOutput").ap()
    qfm_d = nc.dram_tensor("q_fm", [128, 8, NT], BF16, kind="ExternalOutput").ap()
    kfm_d = nc.dram_tensor("k_fm", [128, 8, NT], BF16, kind="ExternalOutput").ap()
    vtm_d = nc.dram_tensor("v_tm", [128, 16, 1040], BF16, kind="ExternalOutput").ap()
    lna_d = nc.dram_tensor("lna_tm", [128, 16 * H], F32, kind="ExternalOutput").ap()

    xt = [p.sbuf(f"xt{i}", [128, D], F32) for i in range(2)]
    stat = [p.sbuf(f"stat{i}", [128, 16], F32) for i in range(2)]
    ubf = [p.sbuf(f"ubf{i}", [128, D], BF16) for i in range(2)]
    u_fm = p.sbuf("ufm", [128, 8, NT], BF16)
    q_fm = p.sbuf("qfm", [128, 8, NT], BF16)
    k_fm = p.sbuf("kfm", [128, 8, NT], BF16)
    v_tm = p.sbuf("vtm", [128, 16, 1040], BF16)
    lna_tm = p.sbuf("lnatm", [128, 16 * H], F32)
    lna_fm = p.sbuf("lnafm", [16, NT], F32)
    a_fm = p.sbuf("afm", [16, NT], F32)
    wmat = p.sbuf("wmat", [128, 16, D], BF16)
    wa_sb = p.sbuf("wasb", [128, 8 * H], BF16)
    ba2_sb = p.sbuf("ba2sb", [16, 1], F32)
    elu_sc = p.sbuf("elusc", [128, 512], F32)
    elu_sc2 = p.sbuf("elusc2", [128, 512], F32)
    ident_bf = p.sbuf("identbf", [128, 128], BF16)
    ident_f = p.sbuf("identf", [128, 128], F32)
    msl = p.sbuf("msl", [128, 128], F32)
    ones_col = p.sbuf("onescol", [128, 1], F32)
    ones_row = p.sbuf("onesrow", [1, 128], F32)
    k_tm = p.sbuf("ktm", [128, D], BF16)
    khat = p.sbuf("khat", [128, D], BF16)
    escale = p.sbuf("escale", [128, H], BF16)
    arow = p.sbuf("arow", [1, H], F32)
    s_run = p.sbuf("srun", [128, 520], F32)
    s_stash = p.sbuf("sstash", [128, 520], F32)

    psB = p.psum("psB", [128, 512], F32)
    psT = p.psum("psT", [128, 1024], BF16)
    psS = p.psum("psS", [128, 1024], F32)
    psR = p.psum("psR", [128, 128], F32)

    SY, TE, VE, SC, GP = ENGINES[0], ENGINES[1], ENGINES[2], ENGINES[3], ENGINES[4]

    make_const(e, GP, ident_bf[:], "identbf", "ident")
    make_const(e, GP, ident_f[:], "identf", "ident")
    make_const(e, GP, msl[:], "msl", "strict_lower")
    make_const(e, GP, ones_col[:], "onescol", "ones")
    make_const(e, GP, ones_row[:], "onesrow", "ones")
    e.op(GP, lambda g: g.memset(
        v_tm[:].rearrange("p h (x y) -> p h x y", y=65)[:, :, :, 64:65], 1.0),
        writes=("vtm_ones",))

    def load_w(dst_half, w_d, tag):
        def fn(g):
            return [g.dma_start(out=wmat[:, dst_half * 8 + dc, :],
                                in_=w_d[dc * 128:(dc + 1) * 128, :]) for dc in range(8)]
        e.op(GP, fn, writes=(tag,), dma=True, n_dma=8)

    load_w(0, wk_d, "wmatA")
    load_w(1, wq_d, "wmatB")

    def fn_wa(g):
        ins = [g.dma_start(out=wa_sb[:, dc * H:(dc + 1) * H],
                           in_=wa_d[dc * 128:(dc + 1) * 128, :]) for dc in range(8)]
        ins.append(g.dma_start(out=ba2_sb[:], in_=ba_d[:].rearrange("(h x) -> h x", x=1)))
        return ins
    e.op(GP, fn_wa, writes=("wasb", "ba2"), dma=True, n_dma=9)
    e.op(VE, lambda v: v.tensor_scalar_add(ba2_sb[:], ba2_sb[:], 2.0),
         reads=("ba2",), writes=("ba2",))

    emit_ln_tiles(e, p, x_d, u_fm,
                  dict(xt=xt, stat=stat, ubf=ubf, psT=psT, ident_bf=ident_bf), ukey="u")

    emit_proj_fm(e, k_fm, "kfm", u_fm,
                 lambda dc, oc: wmat[:, dc, oc * 128:(oc + 1) * 128], ("wmatA",),
                 psB, elu_bufs=(elu_sc, elu_sc2))
    load_w(0, wv_d, "wmatA")
    emit_proj_fm(e, q_fm, "qfm", u_fm,
                 lambda dc, oc: wmat[:, 8 + dc, oc * 128:(oc + 1) * 128], ("wmatB",),
                 psB, elu_bufs=(elu_sc, elu_sc2))

    emit_v_tm(e, v_tm, "vtm", u_fm, "u",
              lambda dc, ng: wmat[:, dc, ng * 512:(ng + 1) * 512], ("wmatA",), psB)

    for tg in range(4):
        def fn_amm(t, tg=tg):
            last = None
            for dc in range(8):
                last = t.matmul(psB[0:16, :], wa_sb[:, dc * H:(dc + 1) * H],
                                u_fm[:, dc, tg * 512:(tg + 1) * 512],
                                start=(dc == 0), stop=(dc == 7))
            return last
        e.op(TE, fn_amm, reads=("wasb", "ufm_u"), writes=("psB",))
        e.op(SC, lambda s, tg=tg: s.activation(
            out=a_fm[:, tg * 512:(tg + 1) * 512], in_=psB[0:16, :], func=AF.Sigmoid,
            bias=ba2_sb[:], scale=1.0), reads=("psB", "ba2"), writes=("afm",))
    e.op(SC, lambda s: s.activation(out=lna_fm[:], in_=a_fm[:], func=AF.Ln),
         reads=("afm",), writes=("lnafm",))

    for grp in range(4):
        def fn_tr(t, grp=grp):
            last = None
            for cc in range(4):
                c = grp * 4 + cc
                last = t.transpose(psR[:, cc * 16:(cc + 1) * 16],
                                   lna_fm[:, c * 128:(c + 1) * 128], ident_f[0:16, 0:16])
            return last
        e.op(TE, fn_tr, reads=("lnafm", "identf"), writes=("psR",))
        e.op(VE, lambda v, grp=grp: v.tensor_copy(
            out=lna_tm[:, grp * 64:(grp + 1) * 64], in_=psR[:, 0:64]),
            reads=("psR",), writes=("lnatm",))

    e.op(VE, lambda v: v.memset(s_run[:], 0.0), writes=("srun",))
    for c in range(16):
        lna_c = lna_tm[:, c * 16:(c + 1) * 16]

        def fn_rev(t, lna_c=lna_c):
            t.matmul(psR[:, 0:16], msl[:], lna_c, start=True, stop=True)
            return t.matmul(psR[0:1, 16:32], ones_col[:], lna_c, start=True, stop=True)
        e.op(TE, fn_rev, reads=("lnatm", "msl", "onescol"), writes=("psR",))
        e.op(SC, lambda s: s.activation(out=escale[:], in_=psR[:, 0:16], func=AF.Exp),
             reads=("psR",), writes=("escale",))
        e.op(SC, lambda s: s.activation(out=arow[:], in_=psR[0:1, 16:32], func=AF.Exp),
             reads=("psR",), writes=("arow",))

        def fn_ktr(t, c=c):
            last = None
            for dc in range(8):
                last = t.transpose(psT[:, dc * 128:(dc + 1) * 128],
                                   k_fm[:, dc, c * 128:(c + 1) * 128], ident_bf[:])
            return last
        e.op(TE, fn_ktr, reads=("kfm", "identbf"), writes=("psT",))
        e.op(VE, lambda v: v.tensor_copy(out=k_tm[:], in_=psT[:]),
             reads=("psT",), writes=("ktm",))
        e.op(VE, lambda v: v.tensor_tensor(
            out=khat[:].rearrange("p (h d) -> p h d", d=64),
            in0=k_tm[:].rearrange("p (h d) -> p h d", d=64),
            in1=escale[:].unsqueeze(2).broadcast_to([128, 16, 64]), op=ALU.mult),
            reads=("ktm", "escale"), writes=("khat",))

        def fn_sadd(t, c=c):
            last = None
            for h in range(16):
                pb = 64 * (h % 2)
                j = h // 2
                off = (j // 4) * 512 + (j % 4) * 65
                last = t.matmul(psS[pb:pb + 64, off:off + 65],
                                khat[:, h * 64:(h + 1) * 64],
                                v_tm[:, c, h * 65:(h + 1) * 65],
                                start=True, stop=True)
            return last
        e.op(TE, fn_sadd, reads=("khat", "vtm", "vtm_ones"), writes=("psS",))

        def fn_abc(t):
            t.matmul(psR[0:64, 32:40], ones_row[0:1, 0:64], arow[:, 0:8],
                     start=True, stop=True)
            return t.matmul(psR[64:128, 32:40], ones_row[0:1, 0:64], arow[:, 8:16],
                            start=True, stop=True)
        e.op(TE, fn_abc, reads=("arow", "onesrow"), writes=("psR",))

        psS_v = psS[:].rearrange("p (b s) -> p b s", s=512)[:, :, 0:260]\
            .rearrange("p b (j x) -> p b j x", x=65)
        e.op(VE, lambda v, psS_v=psS_v: v.tensor_copy(
            out=s_stash[:].rearrange("p (b j x) -> p b j x", b=2, j=4), in_=psS_v),
             reads=("psS",), writes=("sstash",))
        e.op(VE, lambda v: v.tensor_tensor(
            out=s_run[:].rearrange("p (j x) -> p j x", x=65),
            in0=s_run[:].rearrange("p (j x) -> p j x", x=65),
            in1=psR[0:128, 32:40].unsqueeze(2).broadcast_to([128, 8, 65]),
            op=ALU.mult), reads=("srun", "psR"), writes=("srun",))
        e.op(VE, lambda v, psS_v=psS_v: v.tensor_tensor(
            out=s_run[:].rearrange("p (b j x) -> p b j x", b=2, j=4),
            in0=s_run[:].rearrange("p (b j x) -> p b j x", b=2, j=4),
            in1=psS_v, op=ALU.add),
             reads=("srun", "psS"), writes=("srun",))

        e.op(SY, lambda s, c=c: s.dma_start(out=chs_d[c], in_=s_stash[:]),
             reads=("sstash",), dma=True)
        e.op(SY, lambda s, c=c: s.dma_start(out=cha_d[c:c + 1, :], in_=arow[:]),
             reads=("arow",), dma=True)

    def fn_ac(t):
        last = None
        for c in range(16):
            last = t.matmul(psR[0:1, 48:64], ones_col[:], lna_tm[:, c * 16:(c + 1) * 16],
                            start=(c == 0), stop=(c == 15))
        return last
    e.op(TE, fn_ac, reads=("lnatm", "onescol"), writes=("psR",))
    e.op(SC, lambda s: s.activation(out=arow[:], in_=psR[0:1, 48:64], func=AF.Exp),
         reads=("psR",), writes=("arow",))

    e.op(SY, lambda s: s.dma_start(out=send_d[:], in_=s_run[:]), reads=("srun",), dma=True)
    e.op(SY, lambda s: s.dma_start(out=acore_d[:], in_=arow[:]), reads=("arow",), dma=True)
    e.op(SY, lambda s: s.dma_start(out=qfm_d[:], in_=q_fm[:]), reads=("qfm",), dma=True)
    e.op(SY, lambda s: s.dma_start(out=kfm_d[:], in_=k_fm[:]), reads=("kfm",), dma=True)
    e.op(SY, lambda s: s.dma_start(out=vtm_d[:], in_=v_tm[:]),
         reads=("vtm", "vtm_ones"), dma=True)
    e.op(SY, lambda s: s.dma_start(out=lna_d[:], in_=lna_tm[:]), reads=("lnatm",), dma=True)

    e.flush()
    return p


def build_l0b(upto=16, do_prev=True, lvl=99):
    p = Prog()
    nc = p.nc
    e = Emit(p)
    x_d = nc.dram_tensor("x", [NT, D], F32, kind="ExternalInput").ap()
    qfm_d = nc.dram_tensor("q_fm", [128, 8, NT], BF16, kind="ExternalInput").ap()
    kfm_d = nc.dram_tensor("k_fm", [128, 8, NT], BF16, kind="ExternalInput").ap()
    vtm_d = nc.dram_tensor("v_tm", [128, 16, 1040], BF16, kind="ExternalInput").ap()
    lna_d = nc.dram_tensor("lna_tm", [128, 16 * H], F32, kind="ExternalInput").ap()
    sprev_d = nc.dram_tensor("S_prev", [3, 128, 520], F32, kind="ExternalInput").ap()
    aprev_d = nc.dram_tensor("A_prev", [3, H], F32, kind="ExternalInput").ap()
    chs_d = nc.dram_tensor("chunkS", [16, 128, 520], F32, kind="ExternalInput").ap()
    cha_d = nc.dram_tensor("chunkA", [16, H], F32, kind="ExternalInput").ap()
    wo_d = nc.dram_tensor("Wo", [D, D], F32, kind="ExternalInput").ap()

    x1_d = nc.dram_tensor("x1", [NT, D], F32, kind="ExternalOutput").ap()
    u1l_d = nc.dram_tensor("u1last", [1, D], F32, kind="ExternalOutput").ap()

    q_fm = p.sbuf("qfm", [128, 8, NT], BF16)
    k_fm = p.sbuf("kfm", [128, 8, NT], BF16)
    v_tm = p.sbuf("vtm", [128, 16, 1040], BF16)
    lna_tm = p.sbuf("lnatm", [128, 16 * H], F32)
    sprev = p.sbuf("sprev", [128, 3, 520], F32)
    aprev = p.sbuf("aprev", [1, 3 * H], F32)
    chs = [p.sbuf(f"chs{i}", [128, 520], F32) for i in range(2)]
    cha = p.sbuf("cha", [1, 16 * H], F32)
    wo_sb = p.sbuf("wosb", [128, 8, D], BF16)
    s_run = p.sbuf("srun", [128, 520], F32)
    s_run_bf = p.sbuf("srunbf", [128, 520], BF16)
    ident_bf = p.sbuf("identbf", [128, 128], BF16)
    ident_f = p.sbuf("identf", [128, 128], F32)
    mle = p.sbuf("mle", [128, 128], F32)
    msl = p.sbuf("msl", [128, 128], F32)
    mneg = p.sbuf("mneg", [128, 128], F32)
    ones_row = p.sbuf("onesrow", [1, 128], F32)
    lexp_tm = p.sbuf("lexptm", [128, H], F32)
    msl_sc = p.sbuf("mslsc", [128, 256], F32)
    dpair = p.sbuf("dpair", [128, 256], BF16)
    apair = p.sbuf("apair", [128, 256], BF16)
    apair_v = apair[:].rearrange("p (b s) -> p b s", s=128)
    cr_sb = p.sbuf("crsb", [128, 130], F32)
    n_sb = p.sbuf("nsb", [128, 16, 65], F32)
    dr_sb = p.sbuf("drsb", [128, H], F32)
    obuf = p.sbuf("obuf", [128, D], BF16)
    attn_fm = p.sbuf("attnfm", [128, 8, 128], BF16)
    xt = [p.sbuf(f"xt{i}", [128, D], F32) for i in range(2)]
    x1t = p.sbuf("x1t", [128, D], F32)
    u1t = p.sbuf("u1t", [128, D], F32)
    stat = p.sbuf("stat0", [128, 16], F32)

    psBD = p.psum("psBD", [128, 1024], F32)
    psN = p.psum("psN", [128, 1536], F32)
    psY = p.psum("psY", [128, 512], F32)
    psT = p.psum("psT", [128, 1024], BF16)
    psR = p.psum("psR", [128, 512], F32)

    SY, TE, VE, SC, GP = ENGINES

    make_const(e, GP, ident_bf[:], "identbf", "ident")
    make_const(e, GP, ident_f[:], "identf", "ident")
    make_const(e, GP, ones_row[:], "onesrow", "ones")
    make_const(e, GP, msl[:], "msl", "strict_lower")
    e.op(GP, lambda g: g.memset(mle[:], 1.0), writes=("mle",))
    e.op(GP, lambda g: g.affine_select(out=mle[:], in_=mle[:], compare_op=ALU.is_ge,
                                       fill=0.0, base=0, pattern=[[1, 128]],
                                       channel_multiplier=-1),
         reads=("mle",), writes=("mle",))
    e.op(GP, lambda g: g.memset(mneg[:], 0.0), writes=("mneg",))
    e.op(GP, lambda g: g.affine_select(out=mneg[:], in_=mneg[:], compare_op=ALU.is_ge,
                                       fill=-1e30, base=0, pattern=[[1, 128]],
                                       channel_multiplier=-1),
         reads=("mneg",), writes=("mneg",))

    e.op(SY, lambda s: s.dma_start(out=q_fm[:], in_=qfm_d[:]), writes=("qfm",), dma=True)
    e.op(SY, lambda s: s.dma_start(out=k_fm[:], in_=kfm_d[:]), writes=("kfm",), dma=True)
    e.op(SY, lambda s: s.dma_start(out=v_tm[:], in_=vtm_d[:]), writes=("vtm",), dma=True)
    e.op(SY, lambda s: s.dma_start(out=lna_tm[:], in_=lna_d[:]), writes=("lnatm",), dma=True)
    e.op(SY, lambda s: s.dma_start(out=sprev[:], in_=sprev_d[:].rearrange("k p x -> p k x")),
         writes=("sprev",), dma=True)
    e.op(SY, lambda s: s.dma_start(out=aprev[:], in_=aprev_d[:].rearrange("k h -> (k h)").rearrange("(o n) -> o n", o=1)), writes=("aprev",), dma=True)
    e.op(SY, lambda s: s.dma_start(out=cha[:], in_=cha_d[:].rearrange("k h -> (k h)").rearrange("(o n) -> o n", o=1)), writes=("cha",), dma=True)
    e.op(GP, lambda g: g.dma_start(out=wo_sb[:],
                                   in_=wo_d[:].rearrange("(a p) b -> p a b", p=128)),
         writes=("wosb",), dma=True)

    def emit_abc(src_row, tag, extra=()):
        def fn(t):
            t.matmul(psR[0:64, 16:24], ones_row[0:1, 0:64], src_row[:, 0:16:2],
                     start=True, stop=True)
            return t.matmul(psR[64:128, 16:24], ones_row[0:1, 0:64], src_row[:, 1:16:2],
                            start=True, stop=True)
        e.op(TE, fn, reads=(tag, "onesrow") + tuple(extra), writes=("psR_abc",))

    e.op(VE, lambda v: v.tensor_copy(out=s_run[:], in_=sprev[:, 0, :]),
         reads=("sprev",), writes=("srun",))
    for k in range(1, 3 if do_prev else 1):
        emit_abc(aprev[0:1, k * 16:(k + 1) * 16], "aprev")
        e.op(VE, lambda v: v.tensor_tensor(
            out=s_run[:].rearrange("p (j x) -> p j x", x=65),
            in0=s_run[:].rearrange("p (j x) -> p j x", x=65),
            in1=psR[0:128, 16:24].unsqueeze(2).broadcast_to([128, 8, 65]),
            op=ALU.mult), reads=("srun", "psR_abc"), writes=("srun",))
        e.op(VE, lambda v, k=k: v.tensor_tensor(out=s_run[:], in0=s_run[:],
                                                in1=sprev[:, k, :], op=ALU.add),
             reads=("srun", "sprev"), writes=("srun",))

    for c in range(upto):
        lna_c = lna_tm[:, c * 16:(c + 1) * 16]
        e.op(SY, lambda s, c=c: s.dma_start(out=chs[c % 2][:], in_=chs_d[c]),
             writes=(f"chs{c % 2}",), dma=True)

        e.op(VE, lambda v: v.tensor_copy(out=s_run_bf[:], in_=s_run[:]),
             reads=("srun",), writes=("srunbf",))

        def fn_l(t, lna_c=lna_c):
            return t.matmul(psR[:, 0:16], mle[:], lna_c, start=True, stop=True)
        e.op(TE, fn_l, reads=("lnatm", "mle"), writes=("psR_l",))
        e.op(SC, lambda s: s.activation(out=lexp_tm[:], in_=psR[:, 0:16], func=AF.Exp),
             reads=("psR_l",), writes=("lexptm",))

        for hp in range(8 if lvl >= 2 else 0):
            def fn_msc(v, hp=hp, lna_c=lna_c):
                return [v.tensor_scalar(out=msl_sc[:, j * 128:(j + 1) * 128], in0=msl[:],
                                        scalar1=lna_c[:, 2 * hp + j:2 * hp + j + 1],
                                        scalar2=None, op0=ALU.mult) for j in range(2)]
            e.op(VE, fn_msc, reads=("msl", "lnatm"), writes=("mslsc",))

            def fn_darg(t, hp=hp):
                last = []
                for j in range(2):
                    t.matmul(psBD[:, 256 + j * 512:384 + j * 512],
                             msl_sc[:, j * 128:(j + 1) * 128], mle[:],
                             start=True, stop=False)
                    last.append(t.matmul(psBD[:, 256 + j * 512:384 + j * 512],
                                         ident_f[:], mneg[:], start=False, stop=True))
                return last[-1]
            e.op(TE, fn_darg, reads=("mslsc", "mle", "mneg", "identf"), writes=("psBD",))

            def fn_dexp(s, hp=hp):
                return [s.activation(out=dpair[:, j * 128:(j + 1) * 128],
                                     in_=psBD[:, 256 + j * 512:384 + j * 512], func=AF.Exp)
                        for j in range(2)]
            e.op(SC, fn_dexp, reads=("psBD",), writes=("dpair",))

            def fn_qk(t, c=c, hp=hp):
                last = []
                for s in range(2):
                    pb = 64 * s
                    last.append(t.matmul(
                        psBD[:, s * 512:s * 512 + 128],
                        k_fm[pb:pb + 64, hp, c * 128:(c + 1) * 128],
                        q_fm[pb:pb + 64, hp, c * 128:(c + 1) * 128],
                        start=True, stop=True))
                return last
            e.op(TE, fn_qk, reads=("kfm", "qfm"), writes=("psBD",))
            psBD_qk = psBD[:].rearrange("p (b s) -> p b s", s=512)[:, :, 0:128]
            dpair_v = dpair[:].rearrange("p (b s) -> p b s", s=128)
            e.op(VE, lambda v: v.tensor_tensor(out=apair_v, in0=psBD_qk,
                                               in1=dpair_v, op=ALU.mult),
                 reads=("psBD", "dpair"), writes=("apair",))

            if lvl < 3:
                continue
            def fn_n(t, c=c, hp=hp):
                last = []
                for s in range(2):
                    h = 2 * hp + s
                    t.matmul(psN[:, s * 65:(s + 1) * 65], apair[:, s * 128:(s + 1) * 128],
                             v_tm[:, c, h * 65:(h + 1) * 65], start=True, stop=True)
                    pb = 64 * s
                    last.append(t.matmul(
                        psN[:, 512 * (s + 1):512 * (s + 1) + 65],
                        q_fm[pb:pb + 64, hp, c * 128:(c + 1) * 128],
                        s_run_bf[pb:pb + 64, hp * 65:(hp + 1) * 65],
                        start=True, stop=True))
                return last
            e.op(TE, fn_n, reads=("apair", "vtm", "qfm", "srunbf"), writes=("psN",))

            def fn_cr(s):
                return [s.copy(out=cr_sb[:, j * 65:(j + 1) * 65],
                               in_=psN[:, 512 * (j + 1):512 * (j + 1) + 65])
                        for j in range(2)]
            e.op(SC, fn_cr, reads=("psN",), writes=("crsb",))

            def fn_comb(v, hp=hp):
                return [v.scalar_tensor_tensor(
                    out=n_sb[:, 2 * hp + s, :], in0=cr_sb[:, s * 65:(s + 1) * 65],
                    scalar=lexp_tm[:, 2 * hp + s:2 * hp + s + 1],
                    in1=psN[:, s * 65:(s + 1) * 65], op0=ALU.mult, op1=ALU.add)
                    for s in range(2)]
            e.op(VE, fn_comb, reads=("crsb", "lexptm", "psN"), writes=("nsb",))

        if lvl < 4:
            continue
        e.op(VE, lambda v: v.tensor_scalar_add(dr_sb[:], n_sb[:, :, 64], float(GLA_EPS)),
             reads=("nsb",), writes=("drsb",))
        e.op(VE, lambda v: v.reciprocal(out=dr_sb[:], in_=dr_sb[:]),
             reads=("drsb",), writes=("drsb",))
        e.op(VE, lambda v: v.tensor_tensor(
            out=obuf[:].rearrange("p (h d) -> p h d", d=64),
            in0=n_sb[:, :, 0:64],
            in1=dr_sb[:].unsqueeze(2).broadcast_to([128, 16, 64]), op=ALU.mult),
            reads=("nsb", "drsb"), writes=("obuf",))

        emit_abc(cha[0:1, c * 16:(c + 1) * 16], "cha")
        e.op(VE, lambda v: v.tensor_tensor(
            out=s_run[:].rearrange("p (j x) -> p j x", x=65),
            in0=s_run[:].rearrange("p (j x) -> p j x", x=65),
            in1=psR[0:128, 16:24].unsqueeze(2).broadcast_to([128, 8, 65]),
            op=ALU.mult), reads=("srun", "psR_abc"), writes=("srun",))
        e.op(VE, lambda v, c=c: v.tensor_tensor(out=s_run[:], in0=s_run[:],
                                                in1=chs[c % 2][:], op=ALU.add),
             reads=("srun", f"chs{c % 2}"), writes=("srun",))

        def fn_tr(t):
            last = None
            for dc in range(8):
                last = t.transpose(psT[:, dc * 128:(dc + 1) * 128],
                                   obuf[:, dc * 128:(dc + 1) * 128], ident_bf[:])
            return last
        e.op(TE, fn_tr, reads=("obuf", "identbf"), writes=("psT",))
        e.op(VE, lambda v: v.tensor_copy(
            out=attn_fm[:], in_=psT[:].rearrange("p (a b) -> p a b", b=128)),
            reads=("psT",), writes=("attnfm",))

        e.op(SY, lambda s, c=c: s.dma_start(out=xt[c % 2][:],
                                            in_=x_d[c * 128:(c + 1) * 128, :]),
             writes=(f"xt{c % 2}",), dma=True)

        for ng in range(2):
            def fn_wo(t, ng=ng):
                last = None
                for pc in range(8):
                    last = t.matmul(psY[:], attn_fm[:, pc, :],
                                    wo_sb[:, pc, ng * 512:(ng + 1) * 512],
                                    start=(pc == 0), stop=(pc == 7))
                return last
            e.op(TE, fn_wo, reads=("attnfm", "wosb"), writes=("psY",))
            e.op(VE, lambda v, c=c, ng=ng: v.tensor_tensor(
                out=x1t[:, ng * 512:(ng + 1) * 512],
                in0=xt[c % 2][:, ng * 512:(ng + 1) * 512],
                in1=psY[:], op=ALU.add),
                reads=(f"xt{c % 2}", "psY"), writes=("x1t",))
        e.op(SY, lambda s, c=c: s.dma_start(out=x1_d[c * 128:(c + 1) * 128, :],
                                            in_=x1t[:]),
             reads=("x1t",), dma=True, dkey="x1out")

        if c == 15:
            e.op(VE, lambda v: v.bn_stats(out=stat[:, 0:6], in_=x1t[:, 0:512]),
                 reads=("x1t",), writes=("stat0",))
            e.op(VE, lambda v: v.bn_stats(out=stat[:, 6:12], in_=x1t[:, 512:1024]),
                 reads=("x1t",), writes=("stat0",))
            e.op(VE, lambda v: v.bn_aggr(out=stat[:, 12:14], in_=stat[:, 0:12]),
                 reads=("stat0",), writes=("stat0",))
            e.op(VE, lambda v: v.tensor_scalar_add(stat[:, 13:14], stat[:, 13:14],
                                                   float(LN_EPS)),
                 reads=("stat0",), writes=("stat0",))
            e.op(SC, lambda s: s.activation(out=stat[:, 14:15], in_=stat[:, 13:14],
                                            func=AF.Ln), reads=("stat0",), writes=("stat0",))
            e.op(SC, lambda s: s.activation(out=stat[:, 15:16], in_=stat[:, 14:15],
                                            func=AF.Exp, scale=-0.5),
                 reads=("stat0",), writes=("stat0",))
            e.op(VE, lambda v: v.tensor_scalar(out=u1t[:], in0=x1t[:],
                                               scalar1=stat[:, 12:13],
                                               scalar2=stat[:, 15:16],
                                               op0=ALU.subtract, op1=ALU.mult),
                 reads=("x1t", "stat0"), writes=("u1t",))
            e.op(SY, lambda s: s.dma_start(out=u1l_d[:], in_=u1t[127:128, :]),
                 reads=("u1t",), dma=True, dkey="u1out")

    e.flush()
    return p


def build_l0f():
    p = Prog()
    nc = p.nc
    e = Emit(p)
    x1_d = nc.dram_tensor("x1", [NT, D], F32, kind="ExternalInput").ap()
    lv_d = nc.dram_tensor("lastvec", [1, D], F32, kind="ExternalInput").ap()
    wp1_d = nc.dram_tensor("Wp1", [D, 128], F32, kind="ExternalInput").ap()
    bp1_d = nc.dram_tensor("bp1", [128], F32, kind="ExternalInput").ap()
    wp2_d = nc.dram_tensor("Wp2", [128, 8], F32, kind="ExternalInput").ap()
    bp2_d = nc.dram_tensor("bp2", [8], F32, kind="ExternalInput").ap()
    w1_d = nc.dram_tensor("W1", [8, D, 2048], F32, kind="ExternalInput").ap()
    b1_d = nc.dram_tensor("b1", [8, 2048], F32, kind="ExternalInput").ap()
    w2_d = nc.dram_tensor("W2", [8, 2048, D], F32, kind="ExternalInput").ap()
    b2_d = nc.dram_tensor("b2", [8, D], F32, kind="ExternalInput").ap()
    wo_d = nc.dram_tensor("Wo", [D, D], F32, kind="ExternalInput").ap()
    bo_d = nc.dram_tensor("bo", [D], F32, kind="ExternalInput").ap()
    wk_d = nc.dram_tensor("Wk", [D, D], F32, kind="ExternalInput").ap()
    wv_d = nc.dram_tensor("Wv", [D, D], F32, kind="ExternalInput").ap()

    x2_d = nc.dram_tensor("x2", [NT, D], F32, kind="ExternalOutput").ap()
    u2fm_d = nc.dram_tensor("u2_fm", [128, 8, NT], BF16, kind="ExternalOutput").ap()
    kfm_d = nc.dram_tensor("K_fm", [128, 8, NT], BF16, kind="ExternalOutput").ap()
    vtm_d = nc.dram_tensor("V_tm", [128, 16, 1040], BF16, kind="ExternalOutput").ap()
    sel_d = nc.dram_tensor("sel", [1, 8], U32, kind="ExternalOutput").ap()

    xt = [p.sbuf(f"xt{i}", [128, D], F32) for i in range(2)]
    stat = [p.sbuf(f"stat{i}", [128, 16], F32) for i in range(2)]
    ubf = [p.sbuf(f"ubf{i}", [128, D], BF16) for i in range(2)]
    u_fm = p.sbuf("ufm", [128, 8, NT], BF16)
    lnstat = p.sbuf("lnstat", [128, 2 * NTILE], F32)
    lrow = p.sbuf("lrow", [1, D], F32)
    wp1_sb = p.sbuf("wp1sb", [128, 8, 128], BF16)
    wp2_sb = p.sbuf("wp2sb", [128, 8], BF16)
    bp1_sb = p.sbuf("bp1sb", [1, 128], F32)
    bp2_sb = p.sbuf("bp2sb", [1, 8], F32)
    glog = p.sbuf("glog", [1, 8], F32)
    gmax = p.sbuf("gmax", [1, 8], F32)
    gsel = p.sbuf("gsel", [1, 8], U32)
    w1_sb = p.sbuf("w1sb", [128, 8, 2048], BF16)
    vw2_big = p.sbuf("vw2big", [128, 16, 1040], BF16)
    w2_sb = vw2_big[:][:, :, 0:1024]
    b1r = p.sbuf("b1r", [1, 2048], F32)
    b2r = p.sbuf("b2r", [1, D], BF16)
    bor = p.sbuf("bor", [1, D], BF16)
    b1c = p.sbuf("b1c", [128, 16], F32)
    wo_sb = p.sbuf("wosb", [128, 8, D], BF16)
    wkt_sb = p.sbuf("wktsb", [128, 8, D], BF16)
    wvt_sb = p.sbuf("wvtsb", [128, 8, D], BF16)
    wtmp = [p.sbuf(f"wtmp{i}", [128, D], BF16) for i in range(2)]
    ident_bf = p.sbuf("identbf", [128, 128], BF16)
    ident_f = p.sbuf("identf", [128, 128], F32)
    ones_row = p.sbuf("onesrow", [1, 128], F32)
    ones_row_bf = p.sbuf("onesrowbf", [1, 128], BF16)
    h_sb = p.sbuf("hsb", [128, 256], BF16)
    r_sb = p.sbuf("rsb", [128, D], BF16)
    r_fm = p.sbuf("rfm", [128, 8, 128], BF16)
    x2t = p.sbuf("x2t", [128, D], F32)
    u1t = p.sbuf("u1t", [128, D], F32)
    v_tm = vw2_big
    k_fm = w1_sb
    u2_fm = u_fm

    psB = p.psum("psB", [128, 512], F32)
    psT = p.psum("psT", [128, 1024], BF16)
    psH = p.psum("psH", [128, 512], F32)
    psR2 = p.psum("psR2", [128, 2048], F32)

    SY, TE, VE, SC, GP = ENGINES

    make_const(e, GP, ident_bf[:], "identbf", "ident")
    make_const(e, GP, ident_f[:], "identf", "ident")
    make_const(e, GP, ones_row[:], "onesrow", "ones")
    make_const(e, GP, ones_row_bf[:], "onesrowbf", "ones")
    e.op(SY, lambda s: s.dma_start(out=lrow[:], in_=lv_d[:]), writes=("lrow",), dma=True)
    e.op(GP, lambda g: g.dma_start(out=wp1_sb[:],
                                   in_=wp1_d[:].rearrange("(a p) f -> p a f", p=128)),
         writes=("wp1",), dma=True)
    e.op(GP, lambda g: g.dma_start(out=wp2_sb[:], in_=wp2_d[:]), writes=("wp2",), dma=True)
    e.op(SY, lambda s: s.dma_start(out=bp1_sb[:],
                                   in_=bp1_d[:].rearrange("(o n) -> o n", o=1)),
         writes=("bp1",), dma=True)
    e.op(SY, lambda s: s.dma_start(out=bp2_sb[:],
                                   in_=bp2_d[:].rearrange("(o n) -> o n", o=1)),
         writes=("bp2",), dma=True)

    def fn_lvt(t):
        last = None
        for dc in range(8):
            last = t.transpose(psB[:, dc:dc + 1], lrow[0:1, dc * 128:(dc + 1) * 128],
                               ident_f[0:1, 0:1])
        return last
    e.op(TE, fn_lvt, reads=("lrow", "identf"), writes=("psB",))
    lv_fm = p.sbuf("lvfm", [128, 8], BF16)
    e.op(VE, lambda v: v.tensor_copy(out=lv_fm[:], in_=psB[:, 0:8]),
         reads=("psB",), writes=("lvfm",))
    def fn_bp1(t):
        return t.transpose(psB[:, 8:9], bp1_sb[:], ident_f[0:1, 0:1])
    e.op(TE, fn_bp1, reads=("bp1", "identf"), writes=("psB2",))
    bp1c = p.sbuf("bp1c", [128, 1], F32)
    e.op(VE, lambda v: v.tensor_copy(out=bp1c[:], in_=psB[:, 8:9]),
         reads=("psB2",), writes=("bp1c",))

    def fn_z1(t):
        last = None
        for dc in range(8):
            last = t.matmul(psH[:, 0:1], wp1_sb[:, dc, :], lv_fm[:, dc:dc + 1],
                            start=(dc == 0), stop=(dc == 7))
        return last
    e.op(TE, fn_z1, reads=("wp1", "lvfm"), writes=("psH",))
    z1g = p.sbuf("z1g", [128, 1], BF16)
    e.op(SC, lambda s: s.activation(out=z1g[:], in_=psH[:, 0:1], func=AF.Gelu,
                                    bias=bp1c[:], scale=1.0),
         reads=("psH", "bp1c"), writes=("z1g",))
    def fn_z2(t):
        return t.matmul(psH[:, 4:12][0:1, :], z1g[:], wp2_sb[:], start=True, stop=True)
    e.op(TE, fn_z2, reads=("z1g", "wp2"), writes=("psH2",))
    e.op(VE, lambda v: v.tensor_add(out=glog[:], in0=psH[0:1, 4:12], in1=bp2_sb[:]),
         reads=("psH2", "bp2"), writes=("glog",))
    e.op(VE, lambda v: v.max(out=gmax[:], in_=glog[:]), reads=("glog",), writes=("gmax",))
    e.op(VE, lambda v: v.max_index(out=gsel[:], in_max=gmax[:], in_values=glog[:]),
         reads=("gmax", "glog"), writes=("gsel",))
    e.op(SY, lambda s: s.dma_start(out=sel_d[:], in_=gsel[:]), reads=("gsel",), dma=True,
         dkey="selout")

    def fn_sel_dma(g):
        reg = g.alloc_register("selreg")
        g.load(reg, gsel[0:1, 0:1])
        dsem = e.dma_sem(GP, "expertw")
        for ex in range(8):
            with g.If_eq(reg, ex):
                g.dma_start(out=w1_sb[:],
                            in_=w1_d[ex].rearrange("(a p) f -> p a f", p=128)
                            ).then_inc(dsem.h, 16)
                g.dma_start(out=w2_sb,
                            in_=w2_d[ex].rearrange("(a p) f -> p a f", p=128)
                            ).then_inc(dsem.h, 16)
                g.dma_start(out=b1r[:],
                            in_=b1_d[ex].rearrange("(o n) -> o n", o=1)
                            ).then_inc(dsem.h, 16)
                g.dma_start(out=b2r[:],
                            in_=b2_d[ex].rearrange("(o n) -> o n", o=1)
                            ).then_inc(dsem.h, 16)
        return None
    e.op(GP, fn_sel_dma, reads=("gsel",), writes=("w1", "w2", "b1r", "b2r"),
         self_inc=64, dkey="expertw")

    for grp in range(2):
        def fn_b1(t, grp=grp):
            last = None
            for j in range(8):
                fc = grp * 8 + j
                last = t.transpose(psB[:, 16 + j:17 + j],
                                   b1r[0:1, fc * 128:(fc + 1) * 128], ident_f[0:1, 0:1])
            return last
        e.op(TE, fn_b1, reads=("b1r", "identf"), writes=("psB3",))
        e.op(VE, lambda v, grp=grp: v.tensor_copy(out=b1c[:, grp * 8:(grp + 1) * 8],
                                                  in_=psB[:, 16:24]),
             reads=("psB3",), writes=("b1c",))

    e.op(GP, lambda g: g.dma_start(out=wo_sb[:],
                                   in_=wo_d[:].rearrange("(a p) b -> p a b", p=128)),
         writes=("wosb",), dma=True)
    e.op(GP, lambda g: g.dma_start(out=bor[:], in_=bo_d[:].rearrange("(o n) -> o n", o=1)),
         writes=("bor",), dma=True)

    def load_transposed(w_d, dst_sb, dkey, tag):
        for nc_i in range(8):
            wb, wkk = wtmp[nc_i % 2], f"wtmp{nc_i % 2}"
            e.op(GP, lambda g, nc_i=nc_i, wb=wb: g.dma_start(
                out=wb[:], in_=w_d[nc_i * 128:(nc_i + 1) * 128, :]),
                writes=(wkk,), dma=True, dkey=dkey + str(nc_i % 2))
            def fn_tr(t, wb=wb):
                last = None
                for mc in range(8):
                    last = t.transpose(psT[:, mc * 128:(mc + 1) * 128],
                                       wb[:, mc * 128:(mc + 1) * 128], ident_bf[:])
                return last
            e.op(TE, fn_tr, reads=(wkk, "identbf"), writes=("psT",))
            e.op(VE, lambda v, nc_i=nc_i: v.tensor_copy(
                out=dst_sb[:].rearrange("p a (b f) -> p a b f", b=8)[:, :, nc_i, :],
                in_=psT[:].rearrange("p (a b) -> p a b", b=128)),
                reads=("psT",), writes=(tag,))

    load_transposed(wk_d, wkt_sb, "wkload", "wkt")
    load_transposed(wv_d, wvt_sb, "wvload", "wvt")

    emit_ln_tiles(e, p, x1_d, u_fm,
                  dict(xt=xt, stat=stat, ubf=ubf, psT=psT, ident_bf=ident_bf),
                  stat_out=lnstat, ukey="u")

    for tp in range(8):
        for fc in range(16):
            def fn_h(t, tp=tp, fc=fc):
                last = None
                for dc in range(8):
                    last = t.matmul(psH[:, 0:256], w1_sb[:, dc, fc * 128:(fc + 1) * 128],
                                    u_fm[:, dc, tp * 256:(tp + 1) * 256],
                                    start=(dc == 0), stop=(dc == 7))
                return last
            e.op(TE, fn_h, reads=("w1", "ufm_u"), writes=("psH",))
            e.op(SC, lambda s, fc=fc: s.activation(out=h_sb[:], in_=psH[:, 0:256],
                                                   func=AF.Gelu,
                                                   bias=b1c[:, fc:fc + 1], scale=1.0),
                 reads=("psH", "b1c"), writes=("hsb",))

            def fn_r(t, tp=tp, fc=fc):
                last = []
                for ti in range(2):
                    for ng in range(2):
                        last.append(t.matmul(
                            psR2[:, ti * 1024 + ng * 512:ti * 1024 + (ng + 1) * 512],
                            h_sb[:, ti * 128:(ti + 1) * 128],
                            w2_sb[:, fc, ng * 512:(ng + 1) * 512],
                            start=(fc == 0), stop=False))
                return last[-1]
            e.op(TE, fn_r, reads=("hsb", "w2"), writes=("psR2",))
        def fn_rb(t, tp=tp):
            last = []
            for ti in range(2):
                for ng in range(2):
                    last.append(t.matmul(
                        psR2[:, ti * 1024 + ng * 512:ti * 1024 + (ng + 1) * 512],
                        ones_row_bf[:], b2r[0:1, ng * 512:(ng + 1) * 512],
                        start=False, stop=True))
            return last[-1]
        e.op(TE, fn_rb, reads=("onesrowbf", "b2r"), writes=("psR2",))

        for ti in range(2):
            e.op(VE, lambda v, ti=ti: v.tensor_copy(out=r_sb[:],
                                                    in_=psR2[:, ti * 1024:(ti + 1) * 1024]),
                 reads=("psR2",), writes=("rsb",))

            def fn_tr(t):
                last = None
                for dc in range(8):
                    last = t.transpose(psT[:, dc * 128:(dc + 1) * 128],
                                       r_sb[:, dc * 128:(dc + 1) * 128], ident_bf[:])
                return last
            e.op(TE, fn_tr, reads=("rsb", "identbf"), writes=("psT",))
            e.op(VE, lambda v: v.tensor_copy(
                out=r_fm[:], in_=psT[:].rearrange("p (a b) -> p a b", b=128)),
                reads=("psT",), writes=("rfm",))

            i = tp * 2 + ti
            e.op(SY, lambda s, i=i: s.dma_start(out=xt[i % 2][:],
                                                in_=x1_d[i * 128:(i + 1) * 128, :]),
                 writes=(f"xt{i % 2}",), dma=True)
            for ng in range(2):
                def fn_o(t, ng=ng):
                    last = None
                    for pc in range(8):
                        last = t.matmul(psH[:, 0:512], r_fm[:, pc, :],
                                        wo_sb[:, pc, ng * 512:(ng + 1) * 512],
                                        start=(pc == 0), stop=False)
                    return t.matmul(psH[:, 0:512], ones_row_bf[:],
                                    bor[0:1, ng * 512:(ng + 1) * 512],
                                    start=False, stop=True)
                e.op(TE, fn_o, reads=("rfm", "wosb", "onesrowbf", "bor"), writes=("psH",))
                e.op(VE, lambda v, i=i, ng=ng: v.tensor_scalar(
                    out=u1t[:, ng * 512:(ng + 1) * 512],
                    in0=xt[i % 2][:, ng * 512:(ng + 1) * 512],
                    scalar1=lnstat[:, 2 * i:2 * i + 1], scalar2=lnstat[:, 2 * i + 1:2 * i + 2],
                    op0=ALU.subtract, op1=ALU.mult),
                    reads=(f"xt{i % 2}", "lnstat_u"), writes=("u1t",))
                e.op(VE, lambda v, i=i, ng=ng: v.tensor_add(
                    out=u1t[:, ng * 512:(ng + 1) * 512],
                    in0=u1t[:, ng * 512:(ng + 1) * 512],
                    in1=xt[i % 2][:, ng * 512:(ng + 1) * 512]),
                    reads=(f"xt{i % 2}", "u1t"), writes=("u1t",))
                e.op(VE, lambda v, ng=ng: v.scalar_tensor_tensor(
                    out=x2t[:, ng * 512:(ng + 1) * 512], in0=psH[:, 0:512], scalar=0.3,
                    in1=u1t[:, ng * 512:(ng + 1) * 512], op0=ALU.mult, op1=ALU.add),
                    reads=("psH", "u1t"), writes=("x2t",))
            e.op(SY, lambda s, i=i: s.dma_start(out=x2_d[i * 128:(i + 1) * 128, :],
                                                in_=x2t[:]),
                 reads=("x2t",), writes=(f"x2o{i}",), dma=True, dkey="x2out")

    e.op(VE, lambda v: v.engine_nop(), reads=(), writes=("ufm_u", "w1"))
    e.op(GP, lambda g: g.memset(
        v_tm[:].rearrange("p h (x y) -> p h x y", y=65)[:, :, :, 64:65], 1.0),
        writes=("vtm_ones", "w2"))
    emit_ln_tiles(e, p, x2_d, u2_fm,
                  dict(xt=xt, stat=stat, ubf=ubf, psT=psT, ident_bf=ident_bf), ukey="u2",
                  src_key=[f"x2o{i}" for i in range(16)])
    emit_proj_fm(e, k_fm, "kfm", u2_fm,
                 lambda dc, oc: wkt_sb[:, dc, oc * 128:(oc + 1) * 128], ("wkt",),
                 psB, ukey="ufm_u2")
    emit_v_tm(e, v_tm, "vtm", u2_fm, "u2",
              lambda dc, ng: wvt_sb[:, dc, ng * 512:(ng + 1) * 512], ("wvt",), psB)

    e.op(SY, lambda s: s.dma_start(out=u2fm_d[:], in_=u2_fm[:]), reads=("ufm_u2",),
         dma=True, dkey="u2out")
    e.op(SY, lambda s: s.dma_start(out=kfm_d[:], in_=k_fm[:]), reads=("kfm",),
         dma=True, dkey="kout")
    e.op(SY, lambda s: s.dma_start(out=vtm_d[:], in_=v_tm[:]),
         reads=("vtm", "vtm_ones"), dma=True, dkey="vout")
    e.flush()
    return p


def ranges_for(j):
    return [j, 7 - j, 8 + j, 15 - j]


def build_l1a(j_pos):
    p = Prog()
    nc = p.nc
    e = Emit(p)
    u2f_d = nc.dram_tensor("u2_full", [4, 128, 8, NT], BF16, kind="ExternalInput").ap()
    kf_d = nc.dram_tensor("K_full", [4, 128, 8, NT], BF16, kind="ExternalInput").ap()
    vf_d = nc.dram_tensor("V_full", [4, 128, 16, 1040], BF16, kind="ExternalInput").ap()
    wq_d = nc.dram_tensor("Wq", [D, D], F32, kind="ExternalInput").ap()
    wo_d = nc.dram_tensor("Wo", [D, D], F32, kind="ExternalInput").ap()
    y_d = nc.dram_tensor("y", [4, 512, D], F32, kind="ExternalOutput").ap()

    qrs = ranges_for(j_pos)

    wqt_sb = p.sbuf("wqtsb", [128, 8, D], BF16)
    wot_sb = p.sbuf("wotsb", [128, 8, D], BF16)
    wtmp = [p.sbuf(f"wtmp{i}", [128, D], BF16) for i in range(2)]
    ident_bf = p.sbuf("identbf", [128, 128], BF16)
    ones_row = p.sbuf("onesrow", [1, 128], F32)
    ones_col = p.sbuf("onescol", [128, 1], BF16)
    u2r = [p.sbuf(f"u2r{i}", [128, 8, 512], BF16) for i in range(2)]
    q_sb = p.sbuf("qsb", [128, 8, NT], BF16)
    attn_sb = p.sbuf("attnsb", [128, 8, NT], BF16)
    k_sb = [p.sbuf(f"ksb{i}", [128, 4, NT], BF16) for i in range(2)]
    v_sb = [p.sbuf(f"vsb{i}", [128, 4, 16, 130], BF16) for i in range(2)]
    pt_sb = p.sbuf("ptsb", [128, 2048], BF16)
    sq_sc = p.sbuf("sqsc", [128, NT], BF16)
    bmax = p.sbuf("bmax", [1, 64], F32)
    negb = p.sbuf("negb", [128, 1], F32)
    linv = p.sbuf("linv", [1, 512], F32)
    linvb = p.sbuf("linvb", [64, 512], F32)
    y_sb = p.sbuf("ysb", [128, D], F32)

    psS = p.psum("psS", [128, 2048], F32)
    psA = [p.psum("psA0", [128, 512], F32), p.psum("psA1", [128, 512], F32)]
    psB = p.psum("psB", [128, 512], F32)
    psY = p.psum("psY", [128, 512], F32)

    SY, TE, VE, SC, GP = ENGINES

    make_const(e, GP, ident_bf[:], "identbf", "ident")
    make_const(e, GP, ones_row[:], "onesrow", "ones")
    make_const(e, GP, ones_col[:], "onescol", "ones")

    def load_transposed(w_d, dst_sb, dkey, tag):
        for nc_i in range(8):
            wb, wkk = wtmp[nc_i % 2], f"wtmp{nc_i % 2}"
            e.op(GP, lambda g, nc_i=nc_i, wb=wb: g.dma_start(
                out=wb[:], in_=w_d[nc_i * 128:(nc_i + 1) * 128, :]),
                writes=(wkk,), dma=True, dkey=dkey + str(nc_i % 2))

            psS_bf = psS[:].bitcast(BF16)

            def fn_tr(t, wb=wb, psS_bf=psS_bf):
                last = None
                for mc in range(8):
                    last = t.transpose(psS_bf[:, mc * 128:(mc + 1) * 128],
                                       wb[:, mc * 128:(mc + 1) * 128], ident_bf[:])
                return last
            e.op(TE, fn_tr, reads=(wkk, "identbf"), writes=("psS",))
            e.op(VE, lambda v, nc_i=nc_i, psS_bf=psS_bf: v.tensor_copy(
                out=dst_sb[:].rearrange("p a (b f) -> p a b f", b=8)[:, :, nc_i, :],
                in_=psS_bf[:, 0:1024].rearrange("p (a b) -> p a b", b=128)),
                reads=("psS",), writes=(tag,))

    load_transposed(wq_d, wqt_sb, "wqload", "wqt")
    load_transposed(wo_d, wot_sb, "woload", "wot")

    for qi, qr in enumerate(qrs):
        r0, lo = qr // 4, (qr % 4) * 512
        ub, ukk = u2r[qi % 2], f"u2r{qi % 2}"
        e.op(SY, lambda s, r0=r0, lo=lo, ub=ub: s.dma_start(
            out=ub[:], in_=u2f_d[r0][:, :, lo:lo + 512]), writes=(ukk,), dma=True)
        for oc in range(8):
            def fn_mm(t, oc=oc, ub=ub):
                last = None
                for dc in range(8):
                    last = t.matmul(psB[:], wqt_sb[:, dc, oc * 128:(oc + 1) * 128],
                                    ub[:, dc, :], start=(dc == 0), stop=(dc == 7))
                return last
            e.op(TE, fn_mm, reads=("wqt", ukk), writes=("psB",))
            e.op(VE, lambda v, oc=oc, qi=qi: v.tensor_copy(
                out=q_sb[:, oc, qi * 512:(qi + 1) * 512], in_=psB[:]),
                reads=("psB",), writes=("qsb",))

    def emit_norms(src_ap, src_keys, out_off, tag):
        e.op(GP, lambda g: g.tensor_mul(sq_sc[:], src_ap, src_ap),
             reads=src_keys, writes=("sqsc",))

        def fn_nq(t):
            last = []
            for s in range(2):
                for g in range(4):
                    last.append(t.matmul(
                        psS[0:1, s * 1024 + (g % 2) * 512:s * 1024 + (g % 2) * 512 + 512],
                        ones_col[64 * s:64 * s + 64, :],
                        sq_sc[64 * s:64 * s + 64, g * 512:(g + 1) * 512],
                        start=(g < 2), stop=(g >= 2)))
            return last
        e.op(TE, fn_nq, reads=("sqsc", "onescol"), writes=("psS",))

        def fn_red(v):
            v.reduce_max(bmax[:, out_off:out_off + 1], psS[0:1, 0:1024], axis=AX)
            return v.reduce_max(bmax[:, out_off + 1:out_off + 2], psS[0:1, 1024:2048],
                                axis=AX)
        e.op(VE, fn_red, reads=("psS",), writes=(tag,))

    for hp in range(8):
        emit_norms(q_sb[:, hp, :], ("qsb",), 2 * hp, "bmax_q")

    for hp in range(8):
        kb, kk = k_sb[hp % 2], f"ksb{hp % 2}"
        vb, vk = v_sb[hp % 2], f"vsb{hp % 2}"

        def fn_ldk(s, hp=hp, kb=kb):
            return [s.dma_start(out=kb[:, r, :], in_=kf_d[r][:, hp, :]) for r in range(4)]
        e.op(SY, fn_ldk, writes=(kk,), dma=True, n_dma=4, dkey=kk)

        def fn_ldv(s, hp=hp, vb=vb):
            return [s.dma_start(out=vb[:, r], in_=vf_d[r][:, :, 130 * hp:130 * hp + 130])
                    for r in range(4)]
        e.op(SY, fn_ldv, writes=(vk,), dma=True, n_dma=4, dkey=vk)

        for r in range(4):
            emit_norms(kb[:, r, :], (kk,), 32 + 2 * r, "bmax_k")
        e.op(VE, lambda v: v.tensor_max(bmax[:, 40:42], bmax[:, 32:34], bmax[:, 34:36]),
             reads=("bmax_k",), writes=("bmax_k",))
        e.op(VE, lambda v: v.tensor_max(bmax[:, 42:44], bmax[:, 36:38], bmax[:, 38:40]),
             reads=("bmax_k",), writes=("bmax_k",))
        e.op(VE, lambda v: v.tensor_max(bmax[:, 40:42], bmax[:, 40:42], bmax[:, 42:44]),
             reads=("bmax_k",), writes=("bmax_k",))
        e.op(VE, lambda v, hp=hp: v.tensor_mul(bmax[:, 44:46], bmax[:, 40:42],
                                               bmax[:, 2 * hp:2 * hp + 2]),
             reads=("bmax_k", "bmax_q"), writes=("bmax_k",))
        e.op(VE, lambda v: v.tensor_max(bmax[:, 46:47], bmax[:, 44:45], bmax[:, 45:46]),
             reads=("bmax_k",), writes=("bmax_k",))
        e.op(SC, lambda s: s.activation(out=bmax[:, 47:48], in_=bmax[:, 46:47],
                                        func=AF.Sqrt), reads=("bmax_k",), writes=("bmax_k",))
        e.op(VE, lambda v: v.tensor_scalar(out=bmax[:, 48:49], in0=bmax[:, 47:48],
                                           scalar1=-0.1275, scalar2=None, op0=ALU.mult),
             reads=("bmax_k",), writes=("bmax_k",))
        e.op(TE, lambda t: t.matmul(psB[:, 0:1], ones_row[:], bmax[0:1, 48:49],
                                    start=True, stop=True),
             reads=("bmax_k", "onesrow"), writes=("psB",))
        e.op(SC, lambda s: s.copy(out=negb[:], in_=psB[:, 0:1]),
             reads=("psB",), writes=("negb",))

        for qi, qr in enumerate(qrs):
            ntt = 4 * (qr + 1)
            first_pv = True
            for tb0 in range(0, ntt, 2):
                nb = min(2, ntt - tb0)
                def fn_s(t, hp=hp, qi=qi, tb0=tb0, nb=nb, kb=kb):
                    last = []
                    for tb in range(nb):
                        tt = tb0 + tb
                        r, col = tt // 16, (tt % 16) * 128
                        for s in range(2):
                            last.append(t.matmul(
                                psS[:, (tb * 2 + s) * 512:(tb * 2 + s + 1) * 512],
                                kb[64 * s:64 * s + 64, r, col:col + 128],
                                q_sb[64 * s:64 * s + 64, hp, qi * 512:(qi + 1) * 512],
                                start=True, stop=True))
                    return last
                e.op(TE, fn_s, reads=(kk, "qsb"), writes=("psS",))
                e.op(SC, lambda s, nb=nb: s.activation(
                    out=pt_sb[:, 0:nb * 1024], in_=psS[:, 0:nb * 1024], func=AF.Exp,
                    bias=negb[:], scale=0.125),
                    reads=("psS", "negb"), writes=("ptsb",))
                def mask_ops(qr=qr, tb0=tb0, nb=nb):
                    for tb in range(nb):
                        tt = tb0 + tb
                        if 128 * (tt + 1) > 512 * qr:
                            for s in range(2):
                                yield tt, tb, s
                mops = list(mask_ops())
                if mops:
                    def fn_mask(g, mops=mops, qr=qr):
                        last = None
                        for tt, tb, s in mops:
                            base = 512 * qr - 128 * tt
                            last = g.affine_select(
                                out=pt_sb[:, (tb * 2 + s) * 512:(tb * 2 + s + 1) * 512],
                                in_=pt_sb[:, (tb * 2 + s) * 512:(tb * 2 + s + 1) * 512],
                                compare_op=ALU.is_ge, fill=0.0, base=base,
                                pattern=[[1, 512]], channel_multiplier=-1)
                        return last
                    e.op(GP, fn_mask, reads=("ptsb",), writes=("ptsb",))
                def fn_pv(t, hp=hp, tb0=tb0, nb=nb, vb=vb, first=first_pv,
                          last_batch=(tb0 + 2 >= ntt)):
                    out = []
                    for tb in range(nb):
                        tt = tb0 + tb
                        r, ti = tt // 16, tt % 16
                        for s in range(2):
                            out.append(t.matmul(
                                psA[s][0:65, :],
                                vb[:, r, ti, s * 65:(s + 1) * 65],
                                pt_sb[:, (tb * 2 + s) * 512:(tb * 2 + s + 1) * 512],
                                start=(first and tb == 0),
                                stop=(last_batch and tb == nb - 1),
                                skip_group_check=True))
                    return out
                e.op(TE, fn_pv, reads=(vk, "ptsb"), writes=("psA0", "psA1"))
                first_pv = False

            for s in range(2):
                e.op(VE, lambda v, s=s: v.reciprocal(out=linv[:], in_=psA[s][64:65, :]),
                     reads=(f"psA{s}",), writes=("linv",))
                e.op(TE, lambda t: t.matmul(psB[0:64, :], ones_row[0:1, 0:64], linv[:],
                                            start=True, stop=True),
                     reads=("linv", "onesrow"), writes=("psB",))
                e.op(SC, lambda s_: s_.copy(out=linvb[:], in_=psB[0:64, :]),
                     reads=("psB",), writes=("linvb",))
                e.op(VE, lambda v, s=s, hp=hp, qi=qi: v.tensor_tensor(
                    out=attn_sb[64 * s:64 * s + 64, hp, qi * 512:(qi + 1) * 512],
                    in0=psA[s][0:64, :], in1=linvb[:], op=ALU.mult),
                    reads=(f"psA{s}", "linvb"), writes=("attnsb",))

    for qi, qr in enumerate(qrs):
        for ti in range(4):
            for ng in range(2):
                def fn_y(t, qi=qi, ti=ti, ng=ng):
                    last = None
                    for mc in range(8):
                        last = t.matmul(psY[:],
                                        attn_sb[:, mc, qi * 512 + ti * 128:qi * 512 + (ti + 1) * 128],
                                        wot_sb[:, mc, ng * 512:(ng + 1) * 512],
                                        start=(mc == 0), stop=(mc == 7))
                    return last
                e.op(TE, fn_y, reads=("attnsb", "wot"), writes=("psY",))
                e.op(VE, lambda v, ng=ng: v.tensor_copy(out=y_sb[:, ng * 512:(ng + 1) * 512],
                                                        in_=psY[:]),
                     reads=("psY",), writes=("ysb",))
            e.op(SY, lambda s, qi=qi, ti=ti: s.dma_start(
                out=y_d[qi, ti * 128:(ti + 1) * 128, :], in_=y_sb[:]),
                reads=("ysb",), dma=True, dkey="yout")

    e.flush()
    return p


def build_l1f():
    p = Prog()
    nc = p.nc
    e = Emit(p)
    x2_d = nc.dram_tensor("x2", [NT, D], F32, kind="ExternalInput").ap()
    y_d = nc.dram_tensor("y", [NT, D], F32, kind="ExternalInput").ap()
    gw_d = nc.dram_tensor("gate_w", [8, D], F32, kind="ExternalInput").ap()
    w1_d = nc.dram_tensor("w1", [8, D, 2048], F32, kind="ExternalInput").ap()
    w2_d = nc.dram_tensor("w2", [8, 2048, D], F32, kind="ExternalInput").ap()
    w3_d = nc.dram_tensor("w3", [8, D, 2048], F32, kind="ExternalInput").ap()
    sw1_d = nc.dram_tensor("sw1", [1, D, 2048], F32, kind="ExternalInput").ap()
    sw2_d = nc.dram_tensor("sw2", [1, 2048, D], F32, kind="ExternalInput").ap()
    sw3_d = nc.dram_tensor("sw3", [1, D, 2048], F32, kind="ExternalInput").ap()
    x4_d = nc.dram_tensor("x4", [NT, D], F32, kind="ExternalOutput").ap()
    x3_d = nc.dram_tensor("x3s", [NT, D], F32, kind="Internal").ap()

    xt = [p.sbuf(f"xt{i}", [128, D], F32) for i in range(2)]
    stat = [p.sbuf(f"stat{i}", [128, 16], F32) for i in range(2)]
    ubf = [p.sbuf(f"ubf{i}", [128, D], BF16) for i in range(2)]
    yt = [p.sbuf(f"yt{i}", [128, D], F32) for i in range(2)]
    u_fm = p.sbuf("ufm", [128, 8, NT], BF16)
    y_acc = p.sbuf("yacc", [128, 16, D], F32)
    w1h = [p.sbuf(f"w1h{i}", [128, 8, D], BF16) for i in range(2)]
    w3h = [p.sbuf(f"w3h{i}", [128, 8, D], BF16) for i in range(2)]
    w2h_t = p.sbuf("w2h", [128, 8, D], BF16)
    w2h = [w2h_t, w2h_t]
    act_buf = p.sbuf("actbuf", [128, 8, 512], BF16)
    s1_sb = p.sbuf("s1sb", [128, 512], BF16)
    gwt_sb = p.sbuf("gwtsb", [128, 8, 8], BF16)
    comb = p.sbuf("comb", [128, 16, 9], F32)
    gz = p.sbuf("gz", [128, 8], F32)
    gm = p.sbuf("gm", [128, 8], F32)
    negm1 = p.sbuf("negm1", [128, 1], F32)
    gsc = p.sbuf("gsc", [128, 8], F32)
    ident_bf = p.sbuf("identbf", [128, 128], BF16)
    x3t = yt[0]
    x4t = yt[1]

    psH1 = p.psum("psH1", [128, 512], F32)
    psH3 = p.psum("psH3", [128, 512], F32)
    psY = [p.psum("psY0", [128, 512], F32), p.psum("psY1", [128, 512], F32)]
    psT = p.psum("psT", [128, 1024], BF16)
    psZ = p.psum("psZ", [128, 512], F32)
    gw_sb = act_buf[:].rearrange("p a b -> p (a b)")[0:8, 0:1024]

    SY, TE, VE, SC, GP = ENGINES
    make_const(e, GP, ident_bf[:], "identbf", "ident")
    e.op(GP, lambda g: g.memset(y_acc[:], 0.0), writes=("yacc",))

    for i in range(NTILE):
        e.op(SY, lambda s, i=i: s.dma_start(out=xt[i % 2][:],
                                            in_=x2_d[i * 128:(i + 1) * 128, :]),
             writes=(f"xt{i % 2}",), dma=True)
        e.op(SY, lambda s, i=i: s.dma_start(out=yt[i % 2][:],
                                            in_=y_d[i * 128:(i + 1) * 128, :]),
             writes=(f"yt{i % 2}",), dma=True, dkey="ytld")
        e.op(VE, lambda v, i=i: v.tensor_add(out=xt[i % 2][:], in0=xt[i % 2][:],
                                             in1=yt[i % 2][:]),
             reads=(f"xt{i % 2}", f"yt{i % 2}"), writes=(f"xt{i % 2}",)),
        e.op(SY, lambda s, i=i: s.dma_start(out=x3_d[i * 128:(i + 1) * 128, :],
                                            in_=xt[i % 2][:]),
             reads=(f"xt{i % 2}",) + ((f"x3o{i - 2}",) if i >= 2 else ()),
             writes=(f"x3o{i}",), dma=True, dkey=f"x3out{i % 2}")

    emit_ln_tiles(e, p, x3_d, u_fm,
                  dict(xt=xt, stat=stat, ubf=ubf, psT=psT, ident_bf=ident_bf), ukey="u",
                  src_key=[f"x3o{i}" for i in range(16)])

    e.op(GP, lambda g: g.dma_start(out=gw_sb, in_=gw_d[:]), writes=("gw",), dma=True)
    for dc in range(8):
        e.op(TE, lambda t, dc=dc: t.transpose(psZ[:, 0:8].bitcast(BF16)[:, 0:8],
                                              gw_sb[:, dc * 128:(dc + 1) * 128],
                                              ident_bf[0:8, 0:8]),
             reads=("gw", "identbf"), writes=("psZ",))
        e.op(VE, lambda v, dc=dc: v.tensor_copy(out=gwt_sb[:, dc, :],
                                                in_=psZ[:, 0:8].bitcast(BF16)[:, 0:8]),
             reads=("psZ",), writes=("gwt",))
    for ti in range(NTILE):
        def fn_z(t, ti=ti):
            last = None
            for dc in range(8):
                last = t.matmul(psZ[:, 16:24], u_fm[:, dc, ti * 128:(ti + 1) * 128],
                                gwt_sb[:, dc, :], start=(dc == 0), stop=(dc == 7))
            return last
        e.op(TE, fn_z, reads=("ufm_u", "gwt"), writes=("psZ2",))
        e.op(VE, lambda v: v.tensor_copy(out=gz[:], in_=psZ[:, 16:24]),
             reads=("psZ2",), writes=("gz",))
        e.op(VE, lambda v: v.max(out=gm[:], in_=gz[:]), reads=("gz",), writes=("gm",))
        e.op(VE, lambda v: v.tensor_scalar(out=negm1[:], in0=gm[:, 0:1], scalar1=-1.0,
                                           scalar2=None, op0=ALU.mult),
             reads=("gm",), writes=("negm1",))
        e.op(SC, lambda s: s.activation(out=gsc[:], in_=gz[:], func=AF.Exp,
                                        bias=negm1[:], scale=1.0),
             reads=("gz", "negm1"), writes=("gsc",))
        e.op(SC, lambda s: s.activation(out=negm1[:], in_=gm[:, 1:2], func=AF.Exp,
                                        bias=negm1[:], scale=1.0),
             reads=("gm", "negm1"), writes=("negm1b",))
        e.op(VE, lambda v: v.tensor_scalar_add(negm1[:], negm1[:], 1.0),
             reads=("negm1b",), writes=("negm1b",))
        e.op(VE, lambda v: v.reciprocal(out=negm1[:], in_=negm1[:]),
             reads=("negm1b",), writes=("negm1b",))
        e.op(VE, lambda v: v.tensor_scalar(out=gsc[:], in0=gsc[:], scalar1=negm1[:],
                                           scalar2=None, op0=ALU.mult),
             reads=("gsc", "negm1b"), writes=("gsc",))
        e.op(VE, lambda v: v.tensor_scalar(out=gm[:], in0=gz[:], scalar1=gm[:, 1:2],
                                           scalar2=None, op0=ALU.is_ge),
             reads=("gz", "gm"), writes=("gmask",))
        e.op(VE, lambda v, ti=ti: v.tensor_mul(comb[:, ti, 0:8], gsc[:], gm[:]),
             reads=("gsc", "gmask"), writes=("comb",))
        e.op(VE, lambda v, ti=ti: v.memset(comb[:, ti, 8:9], 1.0),
             reads=(), writes=("comb",))

    e.op(VE, lambda v: v.engine_nop(), reads=(), writes=("gw",))

    for eh in range(18):
        ex, half = eh // 2, eh % 2
        wb1, wb3, wb2 = w1h[eh % 2], w3h[eh % 2], w2h[eh % 2]
        wk1, wk3, wk2 = f"w1h{eh % 2}", f"w3h{eh % 2}", "w2h"
        if ex < 8:
            s1d = w1_d[ex][:, half * 1024:(half + 1) * 1024]
            s3d = w3_d[ex][:, half * 1024:(half + 1) * 1024]
            s2d = w2_d[ex][half * 1024:(half + 1) * 1024, :]
        else:
            s1d = sw1_d[0][:, half * 1024:(half + 1) * 1024]
            s3d = sw3_d[0][:, half * 1024:(half + 1) * 1024]
            s2d = sw2_d[0][half * 1024:(half + 1) * 1024, :]
        e.op(GP, lambda g, s1d=s1d, wb1=wb1: g.dma_start(
            out=wb1[:], in_=s1d.rearrange("(a p) f -> p a f", p=128)),
            writes=(wk1,), dma=True, dkey=wk1)
        e.op(GP, lambda g, s3d=s3d, wb3=wb3: g.dma_start(
            out=wb3[:], in_=s3d.rearrange("(a p) f -> p a f", p=128)),
            writes=(wk3,), dma=True, dkey=wk3)
        e.op(GP, lambda g, s2d=s2d, wb2=wb2: g.dma_start(
            out=wb2[:], in_=s2d.rearrange("(a p) f -> p a f", p=128)),
            writes=(wk2,), dma=True, dkey=wk2)

        for tq in range(4):
            for fc in range(8):
                def fn_h(t, tq=tq, fc=fc, wb1=wb1, wb3=wb3):
                    last = []
                    for dc in range(8):
                        t.matmul(psH1[:], wb1[:, dc, fc * 128:(fc + 1) * 128],
                                 u_fm[:, dc, tq * 512:(tq + 1) * 512],
                                 start=(dc == 0), stop=(dc == 7))
                    for dc in range(8):
                        last.append(t.matmul(psH3[:], wb3[:, dc, fc * 128:(fc + 1) * 128],
                                             u_fm[:, dc, tq * 512:(tq + 1) * 512],
                                             start=(dc == 0), stop=(dc == 7)))
                    return last[-1]
                e.op(TE, fn_h, reads=(wk1, wk3, "ufm_u"), writes=("psH1", "psH3"))
                e.op(SC, lambda s: s.activation(out=s1_sb[:], in_=psH1[:], func=AF.Silu),
                     reads=("psH1",), writes=("s1sb",))
                e.op(VE, lambda v, fc=fc: v.tensor_mul(act_buf[:, fc, :], s1_sb[:], psH3[:]),
                     reads=("s1sb", "psH3"), writes=("actbuf",))
            for ti in range(4):
                gi = tq * 4 + ti
                for ng in range(2):
                    def fn_y(t, ti=ti, ng=ng, wb2=wb2):
                        last = None
                        for fc in range(8):
                            last = t.matmul(psY[ng][:], act_buf[:, fc, ti * 128:(ti + 1) * 128],
                                            wb2[:, fc, ng * 512:(ng + 1) * 512],
                                            start=(fc == 0), stop=(fc == 7))
                        return last
                    e.op(TE, fn_y, reads=("actbuf", wk2), writes=(f"psY{ng}",))
                    e.op(VE, lambda v, gi=gi, ng=ng, ex=ex: v.scalar_tensor_tensor(
                        out=y_acc[:, gi, ng * 512:(ng + 1) * 512], in0=psY[ng][:],
                        scalar=comb[:, gi, ex:ex + 1],
                        in1=y_acc[:, gi, ng * 512:(ng + 1) * 512],
                        op0=ALU.mult, op1=ALU.add),
                        reads=(f"psY{ng}", "comb", "yacc"), writes=("yacc",))

    for i in range(NTILE):
        e.op(SY, lambda s, i=i: s.dma_start(out=x3t[:], in_=x3_d[i * 128:(i + 1) * 128, :]),
             reads=(f"x3o{i}", "yt0"), writes=("x3t",), dma=True, dkey="x3ld")
        e.op(VE, lambda v, i=i: v.tensor_add(out=x4t[:], in0=x3t[:], in1=y_acc[:, i, :]),
             reads=("x3t", "yacc", "yt1"), writes=("x4t",))
        e.op(SY, lambda s, i=i: s.dma_start(out=x4_d[i * 128:(i + 1) * 128, :], in_=x4t[:]),
             reads=("x4t",), dma=True, dkey="x4out")

    e.flush()
    return p


def shard_x(x_full):
    out = []
    for c in range(NCORES):
        b, j = c // 4, c % 4
        out.append(np.ascontiguousarray(x_full[b, j * NT:(j + 1) * NT, :]))
    return out


def _qr_owner(qr):
    if qr < 4:
        return qr
    if qr < 8:
        return 7 - qr
    if qr < 12:
        return qr - 8
    return 15 - qr


_PROG_CACHE = {}


def _prog(name, builder):
    if name not in _PROG_CACHE:
        _PROG_CACHE[name] = builder()
    return _PROG_CACHE[name]


def kernel(**inputs):
    f32 = lambda a: np.ascontiguousarray(a, dtype=np.float32)
    x = f32(inputs["L0_attn_x"])
    shards = shard_x(x)

    p1 = _prog("l0a", build_l0a)
    w_l0a = {k: f32(inputs["L0_attn_" + k]) for k in ("Wq", "Wk", "Wv", "Wa", "ba")}
    maps1 = [dict(x=shards[c], **w_l0a) for c in range(NCORES)]
    r1 = run_bass_kernel_spmd(p1.nc, maps1, core_ids=list(range(NCORES))).results

    p2 = _prog("l0b", build_l0b)
    maps2 = []
    for c in range(NCORES):
        b, jp = c // 4, c % 4
        S_prev = np.zeros((3, 128, 520), np.float32)
        A_prev = np.ones((3, H), np.float32)
        for k in range(jp):
            pc = b * 4 + k
            S_prev[3 - jp + k] = r1[pc]["S_end"]
            A_prev[3 - jp + k] = r1[pc]["A_core"][0]
        maps2.append(dict(x=shards[c], q_fm=r1[c]["q_fm"], k_fm=r1[c]["k_fm"],
                          v_tm=r1[c]["v_tm"], lna_tm=r1[c]["lna_tm"],
                          S_prev=S_prev, A_prev=A_prev, chunkS=r1[c]["chunkS"],
                          chunkA=r1[c]["chunkA"], Wo=f32(inputs["L0_attn_Wo"])))
    r2 = run_bass_kernel_spmd(p2.nc, maps2, core_ids=list(range(NCORES))).results

    p3 = _prog("l0f", build_l0f)
    w_l0f = {k: f32(inputs["L0_ffn_" + k]) for k in
             ("Wp1", "bp1", "Wp2", "bp2", "W1", "b1", "W2", "b2", "Wo", "bo")}
    maps3 = []
    for c in range(NCORES):
        b = c // 4
        lastvec = r2[b * 4 + 3]["u1last"]
        maps3.append(dict(x1=r2[c]["x1"], lastvec=lastvec,
                          Wk=f32(inputs["L1_attn_Wk"]), Wv=f32(inputs["L1_attn_Wv"]),
                          **w_l0f))
    r3 = run_bass_kernel_spmd(p3.nc, maps3, core_ids=list(range(NCORES))).results

    yparts = {}
    for jp in range(4):
        p4 = _prog(f"l1a{jp}", lambda jp=jp: build_l1a(jp))
        maps4 = []
        for b in range(2):
            grp = [b * 4 + k for k in range(4)]
            u2_full = np.stack([r3[g]["u2_fm"] for g in grp])
            K_full = np.stack([r3[g]["K_fm"] for g in grp])
            V_full = np.stack([r3[g]["V_tm"] for g in grp])
            maps4.append(dict(u2_full=u2_full, K_full=K_full, V_full=V_full,
                              Wq=f32(inputs["L1_attn_Wq"]), Wo=f32(inputs["L1_attn_Wo"])))
        r4 = run_bass_kernel_spmd(p4.nc, maps4, core_ids=[0, 1]).results
        for b in range(2):
            yparts[(b, jp)] = r4[b]["y"]

    p5 = _prog("l1f", build_l1f)
    w_l1f = {k: f32(inputs["L1_ffn_" + k]) for k in
             ("gate_w", "w1", "w2", "w3", "sw1", "sw2", "sw3")}
    maps5 = []
    for c in range(NCORES):
        b, jp = c // 4, c % 4
        y_own = np.empty((NT, D), np.float32)
        for qi in range(4):
            qr = 4 * jp + qi
            jo = _qr_owner(qr)
            slot = ranges_for(jo).index(qr)
            y_own[qi * 512:(qi + 1) * 512] = yparts[(b, jo)][slot]
        maps5.append(dict(x2=r3[c]["x2"], y=y_own, **w_l1f))
    r5 = run_bass_kernel_spmd(p5.nc, maps5, core_ids=list(range(NCORES))).results

    out = np.empty((2, 8192, D), np.float32)
    for c in range(NCORES):
        b, jp = c // 4, c % 4
        out[b, jp * NT:(jp + 1) * NT, :] = r5[c]["x4"]
    return out


# version 02
# speedup vs baseline: 1.9804x; 1.9804x over previous
import numpy as np
from contextlib import ExitStack

import concourse.bass as bass
import concourse.mybir as mybir
from concourse.bass_utils import run_bass_kernel_spmd

F32 = mybir.dt.float32
BF16 = mybir.dt.bfloat16
U32 = mybir.dt.uint32
AF = mybir.ActivationFunctionType
ALU = mybir.AluOpType
AX = mybir.AxisListType.X

D = 1024
NT = 2048
NTILE = 16
H = 16
DH = 64
NCORES = 8
LN_EPS = 1e-5
GLA_EPS = 1e-6

ENGINES = ("sync", "tensor", "vector", "scalar", "gpsimd")


class Ctr:
    __slots__ = ("h", "n", "name")

    def __init__(self, h, name):
        self.h = h
        self.n = 0
        self.name = name


class Prog:
    def __init__(self):
        self.nc = bass.Bass("TRN2", target_bir_lowering=False, debug=False)
        self.stack = ExitStack()
        self._sems = {}

    def sem(self, name):
        if name not in self._sems:
            h = self.stack.enter_context(self.nc.semaphore(name))
            self._sems[name] = Ctr(h, name)
        return self._sems[name]

    def sbuf(self, name, shape, dtype):
        return self.stack.enter_context(self.nc.sbuf_tensor(name, shape, dtype))

    def psum(self, name, shape, dtype):
        return self.stack.enter_context(self.nc.psum_tensor(name, shape, dtype))


class Emit:

    def __init__(self, p):
        self.p = p
        self.pend = {e: [] for e in ENGINES}
        self.eng_ctr = {e: p.sem("c_" + e) for e in ENGINES if e != "sync"}
        self.dma_sems = {}
        self.last_w = {}
        self.readers = {}
        self.hw = {e: {} for e in ENGINES}

    def dma_sem(self, eng, key):
        k = (eng, key)
        if k not in self.dma_sems:
            self.dma_sems[k] = self.p.sem(f"d{len(self.dma_sems)}_{key[:12]}")
        return self.dma_sems[k]

    def op(self, eng, fn, reads=(), writes=(), dma=False, n_dma=1, self_inc=0,
           dkey=None):
        if dma or self_inc:
            if dkey is None:
                dkey = writes[0] if writes else (reads[0] if reads else "anon")
            ctr = self.dma_sem(eng, dkey)
        else:
            ctr = self.eng_ctr[eng]
        waits = {}

        def need(dep):
            if dep is None:
                return
            c, n = dep
            if n <= 0 or self.hw[eng].get(c, 0) >= n:
                return
            waits[c] = max(waits.get(c, 0), n)

        for b in reads:
            need(self.last_w.get(b))
        for b in writes:
            need(self.last_w.get(b))
            for c, n in self.readers.get(b, {}).items():
                need((c, n))

        for c, n in waits.items():
            self.hw[eng][c] = max(self.hw[eng].get(c, 0), n)
        wl = [(c.h, n) for c, n in waits.items()]

        if self_inc:
            ctr.n += self_inc
        else:
            ctr.n += 16 * n_dma if dma else 1
        my_cnt = ctr.n
        inc_k = 16 if (dma or self_inc) else 1

        def closure(e, fn=fn, wl=wl, ctr=ctr, self_inc=self_inc, inc_k=inc_k):
            for (sh, n) in wl:
                e.wait_ge(sh, n)
            last = fn(e)
            if self_inc:
                return
            assert last is not None, "op fn must return its last instruction(s)"
            if not isinstance(last, (list, tuple)):
                last = [last]
            if inc_k == 16:
                for ins in last:
                    ins.then_inc(ctr.h, inc_k)
            else:
                last[-1].then_inc(ctr.h, 1)

        self.pend[eng].append(closure)
        for b in reads:
            self.readers.setdefault(b, {})[ctr] = my_cnt
        for b in writes:
            self.last_w[b] = (ctr, my_cnt)
            self.readers[b] = {}
        return (ctr, my_cnt)

    def flush(self):
        with self.p.nc.Block() as block:
            for e in ENGINES:
                ops = self.pend[e]
                if not ops:
                    continue

                def run_all(eng, ops=ops):
                    for cl in ops:
                        cl(eng)
                getattr(block, e)(run_all)
            final = [(c.h, c.n) for c in
                     list(self.eng_ctr.values()) + list(self.dma_sems.values()) if c.n > 0]

            def tail(eng):
                for (sh, n) in final:
                    eng.wait_ge(sh, n)
            block.sync(tail)
        self.pend = {e: [] for e in ENGINES}


def make_const(e, g_eng, ap, key, kind, sq=128):
    if kind == "ones":
        e.op(g_eng, lambda g: g.memset(ap, 1.0), writes=(key,))
        return
    fill0, cmp = (0.0, ALU.not_equal) if kind == "ident" else (1.0, ALU.is_gt)
    if kind == "ident":
        e.op(g_eng, lambda g: g.memset(ap, 0.0), writes=(key,))
        e.op(g_eng, lambda g: g.affine_select(out=ap, in_=ap, compare_op=ALU.not_equal,
                                              fill=1.0, base=0, pattern=[[-1, sq]],
                                              channel_multiplier=1),
             reads=(key,), writes=(key,))
    else:
        e.op(g_eng, lambda g: g.memset(ap, 1.0), writes=(key,))
        e.op(g_eng, lambda g: g.affine_select(out=ap, in_=ap, compare_op=ALU.is_gt,
                                              fill=0.0, base=0, pattern=[[-1, sq]],
                                              channel_multiplier=1),
             reads=(key,), writes=(key,))


def emit_ln_tiles(e, p, x_d, u_fm, bufs, n_tiles=NTILE, stat_out=None, ukey="u",
                  src_key=None):
    SY, TE, VE, SC = "sync", "tensor", "vector", "scalar"
    xt, stat, ubf, psT, ident_bf = (bufs["xt"], bufs["stat"], bufs["ubf"],
                                    bufs["psT"], bufs["ident_bf"])
    for i in range(n_tiles):
        xb, sb, ub = xt[i % 2], stat[i % 2], ubf[i % 2]
        xk, sk, uk = f"xt{i % 2}", f"stat{i % 2}", f"ubf{i % 2}"
        e.op(SY, lambda s, i=i, xb=xb: s.dma_start(
            out=xb[:], in_=x_d[i * 128:(i + 1) * 128, :]), writes=(xk,), dma=True,
            reads=(() if src_key is None else tuple(src_key)))
        e.op(VE, lambda v, xb=xb, sb=sb: v.bn_stats(out=sb[:, 0:6], in_=xb[:, 0:512]),
             reads=(xk,), writes=(sk,))
        e.op(VE, lambda v, xb=xb, sb=sb: v.bn_stats(out=sb[:, 6:12], in_=xb[:, 512:1024]),
             reads=(xk,), writes=(sk,))
        e.op(VE, lambda v, sb=sb: v.bn_aggr(out=sb[:, 12:14], in_=sb[:, 0:12]),
             reads=(sk,), writes=(sk,))
        e.op(VE, lambda v, sb=sb: v.tensor_scalar_add(sb[:, 13:14], sb[:, 13:14],
                                                      float(LN_EPS)),
             reads=(sk,), writes=(sk,))
        e.op(SC, lambda s, sb=sb: s.activation(out=sb[:, 14:15], in_=sb[:, 13:14],
                                               func=AF.Ln),
             reads=(sk,), writes=(sk,))
        e.op(SC, lambda s, sb=sb: s.activation(out=sb[:, 15:16], in_=sb[:, 14:15],
                                               func=AF.Exp, scale=-0.5),
             reads=(sk,), writes=(sk,))
        e.op(VE, lambda v, xb=xb, sb=sb, ub=ub: v.tensor_scalar(
            out=ub[:], in0=xb[:], scalar1=sb[:, 12:13], scalar2=sb[:, 15:16],
            op0=ALU.subtract, op1=ALU.mult), reads=(xk, sk), writes=(uk,))
        if stat_out is not None:
            e.op(VE, lambda v, sb=sb, i=i: v.tensor_copy(
                out=stat_out[:, 2 * i:2 * i + 2],
                in_=sb[:].rearrange("p (a b) -> p a b", b=1)[:, (12, 15), :].squeeze(2)
                if False else sb[:, 12:16:3]),
                reads=(sk,), writes=("lnstat_" + ukey,))

        def fn_tr(t, ub=ub):
            last = None
            for dc in range(8):
                last = t.transpose(psT[:, dc * 128:(dc + 1) * 128],
                                   ub[:, dc * 128:(dc + 1) * 128], ident_bf[:])
            return last
        e.op(TE, fn_tr, reads=(uk, "identbf"), writes=("psT",))
        e.op(VE, lambda v, i=i: v.tensor_copy(
            out=u_fm[:, :, i * 128:(i + 1) * 128],
            in_=psT[:].rearrange("p (a b) -> p a b", b=128)),
            reads=("psT",), writes=("ufm_" + ukey,))


def emit_proj_fm(e, dst, dk, u_fm, wsel, wkeys, psB, elu_bufs=None,
                 n_tg=4, psb_key="psB", ukey="ufm_u"):
    TE, VE, SC = "tensor", "vector", "scalar"
    for oc in range(8):
        for tg in range(n_tg):
            def fn_mm(t, oc=oc, tg=tg):
                last = None
                for dc in range(8):
                    last = t.matmul(psB[:], wsel(dc, oc),
                                    u_fm[:, dc, tg * 512:(tg + 1) * 512],
                                    start=(dc == 0), stop=(dc == 7))
                return last
            e.op(TE, fn_mm, reads=wkeys + (ukey,), writes=(psb_key,))
            if elu_bufs is not None:
                elu_sc, elu_sc2 = elu_bufs
                e.op(VE, lambda v: v.tensor_scalar(out=elu_sc[:], in0=psB[:], scalar1=0.0,
                                                   scalar2=None, op0=ALU.min),
                     reads=(psb_key,), writes=("elusc",))
                e.op(SC, lambda s: s.activation(out=elu_sc2[:], in_=elu_sc[:], func=AF.Exp),
                     reads=("elusc",), writes=("elusc2",))
                e.op(VE, lambda v, oc=oc, tg=tg: v.scalar_tensor_tensor(
                    out=dst[:, oc, tg * 512:(tg + 1) * 512], in0=psB[:], scalar=0.0,
                    in1=elu_sc2[:], op0=ALU.max, op1=ALU.add),
                    reads=(psb_key, "elusc2"), writes=(dk,))
            else:
                e.op(VE, lambda v, oc=oc, tg=tg: v.tensor_copy(
                    out=dst[:, oc, tg * 512:(tg + 1) * 512], in_=psB[:]),
                    reads=(psb_key,), writes=(dk,))


def emit_v_tm(e, v_tm, vk, u_fm, udk, wsel, wkeys, psB, psb_key="psB"):
    TE, VE = "tensor", "vector"
    for ti in range(NTILE):
        for ng in range(2):
            def fn_mm(t, ti=ti, ng=ng):
                last = None
                for dc in range(8):
                    last = t.matmul(psB[:], u_fm[:, dc, ti * 128:(ti + 1) * 128],
                                    wsel(dc, ng), start=(dc == 0), stop=(dc == 7))
                return last
            e.op(TE, fn_mm, reads=wkeys + ("ufm_" + udk,), writes=(psb_key,))
            e.op(VE, lambda v, ti=ti, ng=ng: v.tensor_copy(
                out=v_tm[:, ti].rearrange("p (h x) -> p h x", x=65)[:, ng * 8:(ng + 1) * 8, 0:64],
                in_=psB[:].rearrange("p (h d) -> p h d", d=64)),
                reads=(psb_key,), writes=(vk,))


def build_l0a():
    p = Prog()
    nc = p.nc
    e = Emit(p)
    x_d = nc.dram_tensor("x", [NT, D], F32, kind="ExternalInput").ap()
    wq_d = nc.dram_tensor("Wq", [D, D], F32, kind="ExternalInput").ap()
    wk_d = nc.dram_tensor("Wk", [D, D], F32, kind="ExternalInput").ap()
    wv_d = nc.dram_tensor("Wv", [D, D], F32, kind="ExternalInput").ap()
    wa_d = nc.dram_tensor("Wa", [D, H], F32, kind="ExternalInput").ap()
    ba_d = nc.dram_tensor("ba", [H], F32, kind="ExternalInput").ap()

    send_d = nc.dram_tensor("S_end", [128, 520], F32, kind="ExternalOutput").ap()
    acore_d = nc.dram_tensor("A_core", [1, H], F32, kind="ExternalOutput").ap()
    chs_d = nc.dram_tensor("chunkS", [16, 128, 520], F32, kind="ExternalOutput").ap()
    cha_d = nc.dram_tensor("chunkA", [16, H], F32, kind="ExternalOutput").ap()
    qfm_d = nc.dram_tensor("q_fm", [128, 8, NT], BF16, kind="ExternalOutput").ap()
    kfm_d = nc.dram_tensor("k_fm", [128, 8, NT], BF16, kind="ExternalOutput").ap()
    vtm_d = nc.dram_tensor("v_tm", [128, 16, 1040], BF16, kind="ExternalOutput").ap()
    lna_d = nc.dram_tensor("lna_tm", [128, 16 * H], F32, kind="ExternalOutput").ap()

    xt = [p.sbuf(f"xt{i}", [128, D], F32) for i in range(2)]
    stat = [p.sbuf(f"stat{i}", [128, 16], F32) for i in range(2)]
    ubf = [p.sbuf(f"ubf{i}", [128, D], BF16) for i in range(2)]
    u_fm = p.sbuf("ufm", [128, 8, NT], BF16)
    q_fm = p.sbuf("qfm", [128, 8, NT], BF16)
    k_fm = p.sbuf("kfm", [128, 8, NT], BF16)
    v_tm = p.sbuf("vtm", [128, 16, 1040], BF16)
    lna_tm = p.sbuf("lnatm", [128, 16 * H], F32)
    lna_fm = p.sbuf("lnafm", [16, NT], F32)
    a_fm = p.sbuf("afm", [16, NT], F32)
    wmat = p.sbuf("wmat", [128, 16, D], BF16)
    wa_sb = p.sbuf("wasb", [128, 8 * H], BF16)
    ba2_sb = p.sbuf("ba2sb", [16, 1], F32)
    elu_sc = p.sbuf("elusc", [128, 512], F32)
    elu_sc2 = p.sbuf("elusc2", [128, 512], F32)
    ident_bf = p.sbuf("identbf", [128, 128], BF16)
    ident_f = p.sbuf("identf", [128, 128], F32)
    msl = p.sbuf("msl", [128, 128], F32)
    ones_col = p.sbuf("onescol", [128, 1], F32)
    ones_row = p.sbuf("onesrow", [1, 128], F32)
    k_tm = p.sbuf("ktm", [128, D], BF16)
    khat = p.sbuf("khat", [128, D], BF16)
    escale = p.sbuf("escale", [128, H], BF16)
    arow = p.sbuf("arow", [1, H], F32)
    s_run = p.sbuf("srun", [128, 520], F32)
    s_stash = p.sbuf("sstash", [128, 520], F32)

    psB = p.psum("psB", [128, 512], F32)
    psT = p.psum("psT", [128, 1024], BF16)
    psS = p.psum("psS", [128, 1024], F32)
    psR = p.psum("psR", [128, 128], F32)

    SY, TE, VE, SC, GP = ENGINES[0], ENGINES[1], ENGINES[2], ENGINES[3], ENGINES[4]

    make_const(e, GP, ident_bf[:], "identbf", "ident")
    make_const(e, GP, ident_f[:], "identf", "ident")
    make_const(e, GP, msl[:], "msl", "strict_lower")
    make_const(e, GP, ones_col[:], "onescol", "ones")
    make_const(e, GP, ones_row[:], "onesrow", "ones")
    e.op(GP, lambda g: g.memset(
        v_tm[:].rearrange("p h (x y) -> p h x y", y=65)[:, :, :, 64:65], 1.0),
        writes=("vtm_ones",))

    def load_w(dst_half, w_d, tag):
        def fn(g):
            return [g.dma_start(out=wmat[:, dst_half * 8 + dc, :],
                                in_=w_d[dc * 128:(dc + 1) * 128, :]) for dc in range(8)]
        e.op(GP, fn, writes=(tag,), dma=True, n_dma=8)

    load_w(0, wk_d, "wmatA")
    load_w(1, wq_d, "wmatB")

    def fn_wa(g):
        ins = [g.dma_start(out=wa_sb[:, dc * H:(dc + 1) * H],
                           in_=wa_d[dc * 128:(dc + 1) * 128, :]) for dc in range(8)]
        ins.append(g.dma_start(out=ba2_sb[:], in_=ba_d[:].rearrange("(h x) -> h x", x=1)))
        return ins
    e.op(GP, fn_wa, writes=("wasb", "ba2"), dma=True, n_dma=9)
    e.op(VE, lambda v: v.tensor_scalar_add(ba2_sb[:], ba2_sb[:], 2.0),
         reads=("ba2",), writes=("ba2",))

    emit_ln_tiles(e, p, x_d, u_fm,
                  dict(xt=xt, stat=stat, ubf=ubf, psT=psT, ident_bf=ident_bf), ukey="u")

    emit_proj_fm(e, k_fm, "kfm", u_fm,
                 lambda dc, oc: wmat[:, dc, oc * 128:(oc + 1) * 128], ("wmatA",),
                 psB, elu_bufs=(elu_sc, elu_sc2))
    load_w(0, wv_d, "wmatA")
    emit_proj_fm(e, q_fm, "qfm", u_fm,
                 lambda dc, oc: wmat[:, 8 + dc, oc * 128:(oc + 1) * 128], ("wmatB",),
                 psB, elu_bufs=(elu_sc, elu_sc2))

    emit_v_tm(e, v_tm, "vtm", u_fm, "u",
              lambda dc, ng: wmat[:, dc, ng * 512:(ng + 1) * 512], ("wmatA",), psB)

    for tg in range(4):
        def fn_amm(t, tg=tg):
            last = None
            for dc in range(8):
                last = t.matmul(psB[0:16, :], wa_sb[:, dc * H:(dc + 1) * H],
                                u_fm[:, dc, tg * 512:(tg + 1) * 512],
                                start=(dc == 0), stop=(dc == 7))
            return last
        e.op(TE, fn_amm, reads=("wasb", "ufm_u"), writes=("psB",))
        e.op(SC, lambda s, tg=tg: s.activation(
            out=a_fm[:, tg * 512:(tg + 1) * 512], in_=psB[0:16, :], func=AF.Sigmoid,
            bias=ba2_sb[:], scale=1.0), reads=("psB", "ba2"), writes=("afm",))
    e.op(SC, lambda s: s.activation(out=lna_fm[:], in_=a_fm[:], func=AF.Ln),
         reads=("afm",), writes=("lnafm",))

    for grp in range(4):
        def fn_tr(t, grp=grp):
            last = None
            for cc in range(4):
                c = grp * 4 + cc
                last = t.transpose(psR[:, cc * 16:(cc + 1) * 16],
                                   lna_fm[:, c * 128:(c + 1) * 128], ident_f[0:16, 0:16])
            return last
        e.op(TE, fn_tr, reads=("lnafm", "identf"), writes=("psR",))
        e.op(VE, lambda v, grp=grp: v.tensor_copy(
            out=lna_tm[:, grp * 64:(grp + 1) * 64], in_=psR[:, 0:64]),
            reads=("psR",), writes=("lnatm",))

    e.op(VE, lambda v: v.memset(s_run[:], 0.0), writes=("srun",))
    for c in range(16):
        lna_c = lna_tm[:, c * 16:(c + 1) * 16]

        def fn_rev(t, lna_c=lna_c):
            t.matmul(psR[:, 0:16], msl[:], lna_c, start=True, stop=True)
            return t.matmul(psR[0:1, 16:32], ones_col[:], lna_c, start=True, stop=True)
        e.op(TE, fn_rev, reads=("lnatm", "msl", "onescol"), writes=("psR",))
        e.op(SC, lambda s: s.activation(out=escale[:], in_=psR[:, 0:16], func=AF.Exp),
             reads=("psR",), writes=("escale",))
        e.op(SC, lambda s: s.activation(out=arow[:], in_=psR[0:1, 16:32], func=AF.Exp),
             reads=("psR",), writes=("arow",))

        def fn_ktr(t, c=c):
            last = None
            for dc in range(8):
                last = t.transpose(psT[:, dc * 128:(dc + 1) * 128],
                                   k_fm[:, dc, c * 128:(c + 1) * 128], ident_bf[:])
            return last
        e.op(TE, fn_ktr, reads=("kfm", "identbf"), writes=("psT",))
        e.op(VE, lambda v: v.tensor_copy(out=k_tm[:], in_=psT[:]),
             reads=("psT",), writes=("ktm",))
        e.op(VE, lambda v: v.tensor_tensor(
            out=khat[:].rearrange("p (h d) -> p h d", d=64),
            in0=k_tm[:].rearrange("p (h d) -> p h d", d=64),
            in1=escale[:].unsqueeze(2).broadcast_to([128, 16, 64]), op=ALU.mult),
            reads=("ktm", "escale"), writes=("khat",))

        def fn_sadd(t, c=c):
            last = None
            for h in range(16):
                pb = 64 * (h % 2)
                j = h // 2
                off = (j // 4) * 512 + (j % 4) * 65
                last = t.matmul(psS[pb:pb + 64, off:off + 65],
                                khat[:, h * 64:(h + 1) * 64],
                                v_tm[:, c, h * 65:(h + 1) * 65],
                                start=True, stop=True)
            return last
        e.op(TE, fn_sadd, reads=("khat", "vtm", "vtm_ones"), writes=("psS",))

        def fn_abc(t):
            t.matmul(psR[0:64, 32:40], ones_row[0:1, 0:64], arow[:, 0:8],
                     start=True, stop=True)
            return t.matmul(psR[64:128, 32:40], ones_row[0:1, 0:64], arow[:, 8:16],
                            start=True, stop=True)
        e.op(TE, fn_abc, reads=("arow", "onesrow"), writes=("psR",))

        psS_v = psS[:].rearrange("p (b s) -> p b s", s=512)[:, :, 0:260]\
            .rearrange("p b (j x) -> p b j x", x=65)
        e.op(VE, lambda v, psS_v=psS_v: v.tensor_copy(
            out=s_stash[:].rearrange("p (b j x) -> p b j x", b=2, j=4), in_=psS_v),
             reads=("psS",), writes=("sstash",))
        e.op(VE, lambda v: v.tensor_tensor(
            out=s_run[:].rearrange("p (j x) -> p j x", x=65),
            in0=s_run[:].rearrange("p (j x) -> p j x", x=65),
            in1=psR[0:128, 32:40].unsqueeze(2).broadcast_to([128, 8, 65]),
            op=ALU.mult), reads=("srun", "psR"), writes=("srun",))
        e.op(VE, lambda v, psS_v=psS_v: v.tensor_tensor(
            out=s_run[:].rearrange("p (b j x) -> p b j x", b=2, j=4),
            in0=s_run[:].rearrange("p (b j x) -> p b j x", b=2, j=4),
            in1=psS_v, op=ALU.add),
             reads=("srun", "psS"), writes=("srun",))

        e.op(SY, lambda s, c=c: s.dma_start(out=chs_d[c], in_=s_stash[:]),
             reads=("sstash",), dma=True)
        e.op(SY, lambda s, c=c: s.dma_start(out=cha_d[c:c + 1, :], in_=arow[:]),
             reads=("arow",), dma=True)

    def fn_ac(t):
        last = None
        for c in range(16):
            last = t.matmul(psR[0:1, 48:64], ones_col[:], lna_tm[:, c * 16:(c + 1) * 16],
                            start=(c == 0), stop=(c == 15))
        return last
    e.op(TE, fn_ac, reads=("lnatm", "onescol"), writes=("psR",))
    e.op(SC, lambda s: s.activation(out=arow[:], in_=psR[0:1, 48:64], func=AF.Exp),
         reads=("psR",), writes=("arow",))

    e.op(SY, lambda s: s.dma_start(out=send_d[:], in_=s_run[:]), reads=("srun",), dma=True)
    e.op(SY, lambda s: s.dma_start(out=acore_d[:], in_=arow[:]), reads=("arow",), dma=True)
    e.op(SY, lambda s: s.dma_start(out=qfm_d[:], in_=q_fm[:]), reads=("qfm",), dma=True)
    e.op(SY, lambda s: s.dma_start(out=kfm_d[:], in_=k_fm[:]), reads=("kfm",), dma=True)
    e.op(SY, lambda s: s.dma_start(out=vtm_d[:], in_=v_tm[:]),
         reads=("vtm", "vtm_ones"), dma=True)
    e.op(SY, lambda s: s.dma_start(out=lna_d[:], in_=lna_tm[:]), reads=("lnatm",), dma=True)

    e.flush()
    return p


def build_l0b(upto=16, do_prev=True, lvl=99):
    p = Prog()
    nc = p.nc
    e = Emit(p)
    x_d = nc.dram_tensor("x", [NT, D], F32, kind="ExternalInput").ap()
    qfm_d = nc.dram_tensor("q_fm", [128, 8, NT], BF16, kind="ExternalInput").ap()
    kfm_d = nc.dram_tensor("k_fm", [128, 8, NT], BF16, kind="ExternalInput").ap()
    vtm_d = nc.dram_tensor("v_tm", [128, 16, 1040], BF16, kind="ExternalInput").ap()
    lna_d = nc.dram_tensor("lna_tm", [128, 16 * H], F32, kind="ExternalInput").ap()
    sprev_d = nc.dram_tensor("S_prev", [3, 128, 520], F32, kind="ExternalInput").ap()
    aprev_d = nc.dram_tensor("A_prev", [3, H], F32, kind="ExternalInput").ap()
    chs_d = nc.dram_tensor("chunkS", [16, 128, 520], F32, kind="ExternalInput").ap()
    cha_d = nc.dram_tensor("chunkA", [16, H], F32, kind="ExternalInput").ap()
    wo_d = nc.dram_tensor("Wo", [D, D], F32, kind="ExternalInput").ap()

    x1_d = nc.dram_tensor("x1", [NT, D], F32, kind="ExternalOutput").ap()
    u1l_d = nc.dram_tensor("u1last", [1, D], F32, kind="ExternalOutput").ap()

    q_fm = p.sbuf("qfm", [128, 8, NT], BF16)
    k_fm = p.sbuf("kfm", [128, 8, NT], BF16)
    v_tm = p.sbuf("vtm", [128, 16, 1040], BF16)
    lna_tm = p.sbuf("lnatm", [128, 16 * H], F32)
    sprev = p.sbuf("sprev", [128, 3, 520], F32)
    aprev = p.sbuf("aprev", [1, 3 * H], F32)
    chs = [p.sbuf(f"chs{i}", [128, 520], F32) for i in range(2)]
    cha = p.sbuf("cha", [1, 16 * H], F32)
    wo_sb = p.sbuf("wosb", [128, 8, D], BF16)
    s_run = p.sbuf("srun", [128, 520], F32)
    s_run_bf = p.sbuf("srunbf", [128, 520], BF16)
    ident_bf = p.sbuf("identbf", [128, 128], BF16)
    ident_f = p.sbuf("identf", [128, 128], F32)
    mle = p.sbuf("mle", [128, 128], F32)
    msl = p.sbuf("msl", [128, 128], F32)
    mneg = p.sbuf("mneg", [128, 128], F32)
    ones_row = p.sbuf("onesrow", [1, 128], F32)
    lexp_tm = p.sbuf("lexptm", [128, H], F32)
    msl_sc = p.sbuf("mslsc", [128, 256], F32)
    dpair = p.sbuf("dpair", [128, 256], BF16)
    apair = p.sbuf("apair", [128, 256], BF16)
    apair_v = apair[:].rearrange("p (b s) -> p b s", s=128)
    cr_sb = p.sbuf("crsb", [128, 130], F32)
    n_sb = p.sbuf("nsb", [128, 16, 65], F32)
    dr_sb = p.sbuf("drsb", [128, H], F32)
    obuf = p.sbuf("obuf", [128, D], BF16)
    attn_fm = p.sbuf("attnfm", [128, 8, 128], BF16)
    xt = [p.sbuf(f"xt{i}", [128, D], F32) for i in range(2)]
    x1t = p.sbuf("x1t", [128, D], F32)
    u1t = p.sbuf("u1t", [128, D], F32)
    stat = p.sbuf("stat0", [128, 16], F32)

    psBD = p.psum("psBD", [128, 1024], F32)
    psN = p.psum("psN", [128, 1536], F32)
    psY = p.psum("psY", [128, 512], F32)
    psT = p.psum("psT", [128, 1024], BF16)
    psR = p.psum("psR", [128, 512], F32)

    SY, TE, VE, SC, GP = ENGINES

    make_const(e, GP, ident_bf[:], "identbf", "ident")
    make_const(e, GP, ident_f[:], "identf", "ident")
    make_const(e, GP, ones_row[:], "onesrow", "ones")
    make_const(e, GP, msl[:], "msl", "strict_lower")
    e.op(GP, lambda g: g.memset(mle[:], 1.0), writes=("mle",))
    e.op(GP, lambda g: g.affine_select(out=mle[:], in_=mle[:], compare_op=ALU.is_ge,
                                       fill=0.0, base=0, pattern=[[1, 128]],
                                       channel_multiplier=-1),
         reads=("mle",), writes=("mle",))
    e.op(GP, lambda g: g.memset(mneg[:], 0.0), writes=("mneg",))
    e.op(GP, lambda g: g.affine_select(out=mneg[:], in_=mneg[:], compare_op=ALU.is_ge,
                                       fill=-1e30, base=0, pattern=[[1, 128]],
                                       channel_multiplier=-1),
         reads=("mneg",), writes=("mneg",))

    e.op(SY, lambda s: s.dma_start(out=q_fm[:], in_=qfm_d[:]), writes=("qfm",), dma=True)
    e.op(SY, lambda s: s.dma_start(out=k_fm[:], in_=kfm_d[:]), writes=("kfm",), dma=True)
    e.op(SY, lambda s: s.dma_start(out=v_tm[:], in_=vtm_d[:]), writes=("vtm",), dma=True)
    e.op(SY, lambda s: s.dma_start(out=lna_tm[:], in_=lna_d[:]), writes=("lnatm",), dma=True)
    e.op(SY, lambda s: s.dma_start(out=sprev[:], in_=sprev_d[:].rearrange("k p x -> p k x")),
         writes=("sprev",), dma=True)
    e.op(SY, lambda s: s.dma_start(out=aprev[:], in_=aprev_d[:].rearrange("k h -> (k h)").rearrange("(o n) -> o n", o=1)), writes=("aprev",), dma=True)
    e.op(SY, lambda s: s.dma_start(out=cha[:], in_=cha_d[:].rearrange("k h -> (k h)").rearrange("(o n) -> o n", o=1)), writes=("cha",), dma=True)
    e.op(GP, lambda g: g.dma_start(out=wo_sb[:],
                                   in_=wo_d[:].rearrange("(a p) b -> p a b", p=128)),
         writes=("wosb",), dma=True)

    def emit_abc(src_row, tag, extra=()):
        def fn(t):
            t.matmul(psR[0:64, 16:24], ones_row[0:1, 0:64], src_row[:, 0:16:2],
                     start=True, stop=True)
            return t.matmul(psR[64:128, 16:24], ones_row[0:1, 0:64], src_row[:, 1:16:2],
                            start=True, stop=True)
        e.op(TE, fn, reads=(tag, "onesrow") + tuple(extra), writes=("psR_abc",))

    e.op(VE, lambda v: v.tensor_copy(out=s_run[:], in_=sprev[:, 0, :]),
         reads=("sprev",), writes=("srun",))
    for k in range(1, 3 if do_prev else 1):
        emit_abc(aprev[0:1, k * 16:(k + 1) * 16], "aprev")
        e.op(VE, lambda v: v.tensor_tensor(
            out=s_run[:].rearrange("p (j x) -> p j x", x=65),
            in0=s_run[:].rearrange("p (j x) -> p j x", x=65),
            in1=psR[0:128, 16:24].unsqueeze(2).broadcast_to([128, 8, 65]),
            op=ALU.mult), reads=("srun", "psR_abc"), writes=("srun",))
        e.op(VE, lambda v, k=k: v.tensor_tensor(out=s_run[:], in0=s_run[:],
                                                in1=sprev[:, k, :], op=ALU.add),
             reads=("srun", "sprev"), writes=("srun",))

    for c in range(upto):
        lna_c = lna_tm[:, c * 16:(c + 1) * 16]
        e.op(SY, lambda s, c=c: s.dma_start(out=chs[c % 2][:], in_=chs_d[c]),
             writes=(f"chs{c % 2}",), dma=True)

        e.op(VE, lambda v: v.tensor_copy(out=s_run_bf[:], in_=s_run[:]),
             reads=("srun",), writes=("srunbf",))

        def fn_l(t, lna_c=lna_c):
            return t.matmul(psR[:, 0:16], mle[:], lna_c, start=True, stop=True)
        e.op(TE, fn_l, reads=("lnatm", "mle"), writes=("psR_l",))
        e.op(SC, lambda s: s.activation(out=lexp_tm[:], in_=psR[:, 0:16], func=AF.Exp),
             reads=("psR_l",), writes=("lexptm",))

        for hp in range(8 if lvl >= 2 else 0):
            def fn_msc(v, hp=hp, lna_c=lna_c):
                return [v.tensor_scalar(out=msl_sc[:, j * 128:(j + 1) * 128], in0=msl[:],
                                        scalar1=lna_c[:, 2 * hp + j:2 * hp + j + 1],
                                        scalar2=None, op0=ALU.mult) for j in range(2)]
            e.op(VE, fn_msc, reads=("msl", "lnatm"), writes=("mslsc",))

            def fn_darg(t, hp=hp):
                last = []
                for j in range(2):
                    t.matmul(psBD[:, 256 + j * 512:384 + j * 512],
                             msl_sc[:, j * 128:(j + 1) * 128], mle[:],
                             start=True, stop=False)
                    last.append(t.matmul(psBD[:, 256 + j * 512:384 + j * 512],
                                         ident_f[:], mneg[:], start=False, stop=True))
                return last[-1]
            e.op(TE, fn_darg, reads=("mslsc", "mle", "mneg", "identf"), writes=("psBD",))

            def fn_dexp(s, hp=hp):
                return [s.activation(out=dpair[:, j * 128:(j + 1) * 128],
                                     in_=psBD[:, 256 + j * 512:384 + j * 512], func=AF.Exp)
                        for j in range(2)]
            e.op(SC, fn_dexp, reads=("psBD",), writes=("dpair",))

            def fn_qk(t, c=c, hp=hp):
                last = []
                for s in range(2):
                    pb = 64 * s
                    last.append(t.matmul(
                        psBD[:, s * 512:s * 512 + 128],
                        k_fm[pb:pb + 64, hp, c * 128:(c + 1) * 128],
                        q_fm[pb:pb + 64, hp, c * 128:(c + 1) * 128],
                        start=True, stop=True))
                return last
            e.op(TE, fn_qk, reads=("kfm", "qfm"), writes=("psBD",))
            psBD_qk = psBD[:].rearrange("p (b s) -> p b s", s=512)[:, :, 0:128]
            dpair_v = dpair[:].rearrange("p (b s) -> p b s", s=128)
            e.op(VE, lambda v: v.tensor_tensor(out=apair_v, in0=psBD_qk,
                                               in1=dpair_v, op=ALU.mult),
                 reads=("psBD", "dpair"), writes=("apair",))

            if lvl < 3:
                continue
            def fn_n(t, c=c, hp=hp):
                last = []
                for s in range(2):
                    h = 2 * hp + s
                    t.matmul(psN[:, s * 65:(s + 1) * 65], apair[:, s * 128:(s + 1) * 128],
                             v_tm[:, c, h * 65:(h + 1) * 65], start=True, stop=True)
                    pb = 64 * s
                    last.append(t.matmul(
                        psN[:, 512 * (s + 1):512 * (s + 1) + 65],
                        q_fm[pb:pb + 64, hp, c * 128:(c + 1) * 128],
                        s_run_bf[pb:pb + 64, hp * 65:(hp + 1) * 65],
                        start=True, stop=True))
                return last
            e.op(TE, fn_n, reads=("apair", "vtm", "qfm", "srunbf"), writes=("psN",))

            def fn_cr(s):
                return [s.copy(out=cr_sb[:, j * 65:(j + 1) * 65],
                               in_=psN[:, 512 * (j + 1):512 * (j + 1) + 65])
                        for j in range(2)]
            e.op(SC, fn_cr, reads=("psN",), writes=("crsb",))

            def fn_comb(v, hp=hp):
                return [v.scalar_tensor_tensor(
                    out=n_sb[:, 2 * hp + s, :], in0=cr_sb[:, s * 65:(s + 1) * 65],
                    scalar=lexp_tm[:, 2 * hp + s:2 * hp + s + 1],
                    in1=psN[:, s * 65:(s + 1) * 65], op0=ALU.mult, op1=ALU.add)
                    for s in range(2)]
            e.op(VE, fn_comb, reads=("crsb", "lexptm", "psN"), writes=("nsb",))

        if lvl < 4:
            continue
        e.op(VE, lambda v: v.tensor_scalar_add(dr_sb[:], n_sb[:, :, 64], float(GLA_EPS)),
             reads=("nsb",), writes=("drsb",))
        e.op(VE, lambda v: v.reciprocal(out=dr_sb[:], in_=dr_sb[:]),
             reads=("drsb",), writes=("drsb",))
        e.op(VE, lambda v: v.tensor_tensor(
            out=obuf[:].rearrange("p (h d) -> p h d", d=64),
            in0=n_sb[:, :, 0:64],
            in1=dr_sb[:].unsqueeze(2).broadcast_to([128, 16, 64]), op=ALU.mult),
            reads=("nsb", "drsb"), writes=("obuf",))

        emit_abc(cha[0:1, c * 16:(c + 1) * 16], "cha")
        e.op(VE, lambda v: v.tensor_tensor(
            out=s_run[:].rearrange("p (j x) -> p j x", x=65),
            in0=s_run[:].rearrange("p (j x) -> p j x", x=65),
            in1=psR[0:128, 16:24].unsqueeze(2).broadcast_to([128, 8, 65]),
            op=ALU.mult), reads=("srun", "psR_abc"), writes=("srun",))
        e.op(VE, lambda v, c=c: v.tensor_tensor(out=s_run[:], in0=s_run[:],
                                                in1=chs[c % 2][:], op=ALU.add),
             reads=("srun", f"chs{c % 2}"), writes=("srun",))

        def fn_tr(t):
            last = None
            for dc in range(8):
                last = t.transpose(psT[:, dc * 128:(dc + 1) * 128],
                                   obuf[:, dc * 128:(dc + 1) * 128], ident_bf[:])
            return last
        e.op(TE, fn_tr, reads=("obuf", "identbf"), writes=("psT",))
        e.op(VE, lambda v: v.tensor_copy(
            out=attn_fm[:], in_=psT[:].rearrange("p (a b) -> p a b", b=128)),
            reads=("psT",), writes=("attnfm",))

        e.op(SY, lambda s, c=c: s.dma_start(out=xt[c % 2][:],
                                            in_=x_d[c * 128:(c + 1) * 128, :]),
             writes=(f"xt{c % 2}",), dma=True)

        for ng in range(2):
            def fn_wo(t, ng=ng):
                last = None
                for pc in range(8):
                    last = t.matmul(psY[:], attn_fm[:, pc, :],
                                    wo_sb[:, pc, ng * 512:(ng + 1) * 512],
                                    start=(pc == 0), stop=(pc == 7))
                return last
            e.op(TE, fn_wo, reads=("attnfm", "wosb"), writes=("psY",))
            e.op(VE, lambda v, c=c, ng=ng: v.tensor_tensor(
                out=x1t[:, ng * 512:(ng + 1) * 512],
                in0=xt[c % 2][:, ng * 512:(ng + 1) * 512],
                in1=psY[:], op=ALU.add),
                reads=(f"xt{c % 2}", "psY"), writes=("x1t",))
        e.op(SY, lambda s, c=c: s.dma_start(out=x1_d[c * 128:(c + 1) * 128, :],
                                            in_=x1t[:]),
             reads=("x1t",), dma=True, dkey="x1out")

        if c == 15:
            e.op(VE, lambda v: v.bn_stats(out=stat[:, 0:6], in_=x1t[:, 0:512]),
                 reads=("x1t",), writes=("stat0",))
            e.op(VE, lambda v: v.bn_stats(out=stat[:, 6:12], in_=x1t[:, 512:1024]),
                 reads=("x1t",), writes=("stat0",))
            e.op(VE, lambda v: v.bn_aggr(out=stat[:, 12:14], in_=stat[:, 0:12]),
                 reads=("stat0",), writes=("stat0",))
            e.op(VE, lambda v: v.tensor_scalar_add(stat[:, 13:14], stat[:, 13:14],
                                                   float(LN_EPS)),
                 reads=("stat0",), writes=("stat0",))
            e.op(SC, lambda s: s.activation(out=stat[:, 14:15], in_=stat[:, 13:14],
                                            func=AF.Ln), reads=("stat0",), writes=("stat0",))
            e.op(SC, lambda s: s.activation(out=stat[:, 15:16], in_=stat[:, 14:15],
                                            func=AF.Exp, scale=-0.5),
                 reads=("stat0",), writes=("stat0",))
            e.op(VE, lambda v: v.tensor_scalar(out=u1t[:], in0=x1t[:],
                                               scalar1=stat[:, 12:13],
                                               scalar2=stat[:, 15:16],
                                               op0=ALU.subtract, op1=ALU.mult),
                 reads=("x1t", "stat0"), writes=("u1t",))
            e.op(SY, lambda s: s.dma_start(out=u1l_d[:], in_=u1t[127:128, :]),
                 reads=("u1t",), dma=True, dkey="u1out")

    e.flush()
    return p


def build_l0f():
    p = Prog()
    nc = p.nc
    e = Emit(p)
    x1_d = nc.dram_tensor("x1", [NT, D], F32, kind="ExternalInput").ap()
    lv_d = nc.dram_tensor("lastvec", [1, D], F32, kind="ExternalInput").ap()
    wp1_d = nc.dram_tensor("Wp1", [D, 128], F32, kind="ExternalInput").ap()
    bp1_d = nc.dram_tensor("bp1", [128], F32, kind="ExternalInput").ap()
    wp2_d = nc.dram_tensor("Wp2", [128, 8], F32, kind="ExternalInput").ap()
    bp2_d = nc.dram_tensor("bp2", [8], F32, kind="ExternalInput").ap()
    w1_d = nc.dram_tensor("W1", [8, D, 2048], F32, kind="ExternalInput").ap()
    b1_d = nc.dram_tensor("b1", [8, 2048], F32, kind="ExternalInput").ap()
    w2_d = nc.dram_tensor("W2", [8, 2048, D], F32, kind="ExternalInput").ap()
    b2_d = nc.dram_tensor("b2", [8, D], F32, kind="ExternalInput").ap()
    wo_d = nc.dram_tensor("Wo", [D, D], F32, kind="ExternalInput").ap()
    bo_d = nc.dram_tensor("bo", [D], F32, kind="ExternalInput").ap()

    x2_d = nc.dram_tensor("x2", [NT, D], F32, kind="ExternalOutput").ap()
    u2fm_d = nc.dram_tensor("u2_fm", [128, 8, NT], BF16, kind="ExternalOutput").ap()
    sel_d = nc.dram_tensor("sel", [1, 8], U32, kind="ExternalOutput").ap()

    xt = [p.sbuf(f"xt{i}", [128, D], F32) for i in range(2)]
    stat = [p.sbuf(f"stat{i}", [128, 16], F32) for i in range(2)]
    ubf = [p.sbuf(f"ubf{i}", [128, D], BF16) for i in range(2)]
    u_fm = p.sbuf("ufm", [128, 8, NT], BF16)
    lnstat = p.sbuf("lnstat", [128, 2 * NTILE], F32)
    lrow = p.sbuf("lrow", [1, D], F32)
    wp1_sb = p.sbuf("wp1sb", [128, 8, 128], BF16)
    wp2_sb = p.sbuf("wp2sb", [128, 8], BF16)
    bp1_sb = p.sbuf("bp1sb", [1, 128], F32)
    bp2_sb = p.sbuf("bp2sb", [1, 8], F32)
    glog = p.sbuf("glog", [1, 8], F32)
    gmax = p.sbuf("gmax", [1, 8], F32)
    gsel = p.sbuf("gsel", [1, 8], U32)
    w1_sb = p.sbuf("w1sb", [128, 8, 2048], BF16)
    vw2_big = p.sbuf("vw2big", [128, 16, 1040], BF16)
    w2_sb = vw2_big[:][:, :, 0:1024]
    b1r = p.sbuf("b1r", [1, 2048], F32)
    b2r = p.sbuf("b2r", [1, D], BF16)
    bor = p.sbuf("bor", [1, D], BF16)
    b1c = p.sbuf("b1c", [128, 16], F32)
    wo_sb = p.sbuf("wosb", [128, 8, D], BF16)
    ident_bf = p.sbuf("identbf", [128, 128], BF16)
    ident_f = p.sbuf("identf", [128, 128], F32)
    ones_row = p.sbuf("onesrow", [1, 128], F32)
    ones_row_bf = p.sbuf("onesrowbf", [1, 128], BF16)
    h_sb = p.sbuf("hsb", [128, 256], BF16)
    r_sb = p.sbuf("rsb", [128, D], BF16)
    r_fm = p.sbuf("rfm", [128, 8, 128], BF16)
    x2t = p.sbuf("x2t", [128, D], F32)
    u1t = p.sbuf("u1t", [128, D], F32)
    u2_fm = u_fm

    psB = p.psum("psB", [128, 512], F32)
    psT = p.psum("psT", [128, 1024], BF16)
    psH = p.psum("psH", [128, 512], F32)
    psR2 = p.psum("psR2", [128, 2048], F32)

    SY, TE, VE, SC, GP = ENGINES

    make_const(e, GP, ident_bf[:], "identbf", "ident")
    make_const(e, GP, ident_f[:], "identf", "ident")
    make_const(e, GP, ones_row[:], "onesrow", "ones")
    make_const(e, GP, ones_row_bf[:], "onesrowbf", "ones")
    e.op(SY, lambda s: s.dma_start(out=lrow[:], in_=lv_d[:]), writes=("lrow",), dma=True)
    e.op(GP, lambda g: g.dma_start(out=wp1_sb[:],
                                   in_=wp1_d[:].rearrange("(a p) f -> p a f", p=128)),
         writes=("wp1",), dma=True)
    e.op(GP, lambda g: g.dma_start(out=wp2_sb[:], in_=wp2_d[:]), writes=("wp2",), dma=True)
    e.op(SY, lambda s: s.dma_start(out=bp1_sb[:],
                                   in_=bp1_d[:].rearrange("(o n) -> o n", o=1)),
         writes=("bp1",), dma=True)
    e.op(SY, lambda s: s.dma_start(out=bp2_sb[:],
                                   in_=bp2_d[:].rearrange("(o n) -> o n", o=1)),
         writes=("bp2",), dma=True)

    def fn_lvt(t):
        last = None
        for dc in range(8):
            last = t.transpose(psB[:, dc:dc + 1], lrow[0:1, dc * 128:(dc + 1) * 128],
                               ident_f[0:1, 0:1])
        return last
    e.op(TE, fn_lvt, reads=("lrow", "identf"), writes=("psB",))
    lv_fm = p.sbuf("lvfm", [128, 8], BF16)
    e.op(VE, lambda v: v.tensor_copy(out=lv_fm[:], in_=psB[:, 0:8]),
         reads=("psB",), writes=("lvfm",))
    def fn_bp1(t):
        return t.transpose(psB[:, 8:9], bp1_sb[:], ident_f[0:1, 0:1])
    e.op(TE, fn_bp1, reads=("bp1", "identf"), writes=("psB2",))
    bp1c = p.sbuf("bp1c", [128, 1], F32)
    e.op(VE, lambda v: v.tensor_copy(out=bp1c[:], in_=psB[:, 8:9]),
         reads=("psB2",), writes=("bp1c",))

    def fn_z1(t):
        last = None
        for dc in range(8):
            last = t.matmul(psH[:, 0:1], wp1_sb[:, dc, :], lv_fm[:, dc:dc + 1],
                            start=(dc == 0), stop=(dc == 7))
        return last
    e.op(TE, fn_z1, reads=("wp1", "lvfm"), writes=("psH",))
    z1g = p.sbuf("z1g", [128, 1], BF16)
    e.op(SC, lambda s: s.activation(out=z1g[:], in_=psH[:, 0:1], func=AF.Gelu,
                                    bias=bp1c[:], scale=1.0),
         reads=("psH", "bp1c"), writes=("z1g",))
    def fn_z2(t):
        return t.matmul(psH[:, 4:12][0:1, :], z1g[:], wp2_sb[:], start=True, stop=True)
    e.op(TE, fn_z2, reads=("z1g", "wp2"), writes=("psH2",))
    e.op(VE, lambda v: v.tensor_add(out=glog[:], in0=psH[0:1, 4:12], in1=bp2_sb[:]),
         reads=("psH2", "bp2"), writes=("glog",))
    e.op(VE, lambda v: v.max(out=gmax[:], in_=glog[:]), reads=("glog",), writes=("gmax",))
    e.op(VE, lambda v: v.max_index(out=gsel[:], in_max=gmax[:], in_values=glog[:]),
         reads=("gmax", "glog"), writes=("gsel",))
    e.op(SY, lambda s: s.dma_start(out=sel_d[:], in_=gsel[:]), reads=("gsel",), dma=True,
         dkey="selout")

    def fn_sel_dma(g):
        reg = g.alloc_register("selreg")
        g.load(reg, gsel[0:1, 0:1])
        dsem = e.dma_sem(GP, "expertw")
        for ex in range(8):
            with g.If_eq(reg, ex):
                g.dma_start(out=w1_sb[:],
                            in_=w1_d[ex].rearrange("(a p) f -> p a f", p=128)
                            ).then_inc(dsem.h, 16)
                g.dma_start(out=w2_sb,
                            in_=w2_d[ex].rearrange("(a p) f -> p a f", p=128)
                            ).then_inc(dsem.h, 16)
                g.dma_start(out=b1r[:],
                            in_=b1_d[ex].rearrange("(o n) -> o n", o=1)
                            ).then_inc(dsem.h, 16)
                g.dma_start(out=b2r[:],
                            in_=b2_d[ex].rearrange("(o n) -> o n", o=1)
                            ).then_inc(dsem.h, 16)
        return None
    e.op(GP, fn_sel_dma, reads=("gsel",), writes=("w1", "w2", "b1r", "b2r"),
         self_inc=64, dkey="expertw")

    for grp in range(2):
        def fn_b1(t, grp=grp):
            last = None
            for j in range(8):
                fc = grp * 8 + j
                last = t.transpose(psB[:, 16 + j:17 + j],
                                   b1r[0:1, fc * 128:(fc + 1) * 128], ident_f[0:1, 0:1])
            return last
        e.op(TE, fn_b1, reads=("b1r", "identf"), writes=("psB3",))
        e.op(VE, lambda v, grp=grp: v.tensor_copy(out=b1c[:, grp * 8:(grp + 1) * 8],
                                                  in_=psB[:, 16:24]),
             reads=("psB3",), writes=("b1c",))

    e.op(GP, lambda g: g.dma_start(out=wo_sb[:],
                                   in_=wo_d[:].rearrange("(a p) b -> p a b", p=128)),
         writes=("wosb",), dma=True)
    e.op(GP, lambda g: g.dma_start(out=bor[:], in_=bo_d[:].rearrange("(o n) -> o n", o=1)),
         writes=("bor",), dma=True)

    emit_ln_tiles(e, p, x1_d, u_fm,
                  dict(xt=xt, stat=stat, ubf=ubf, psT=psT, ident_bf=ident_bf),
                  stat_out=lnstat, ukey="u")

    for tp in range(8):
        for fc in range(16):
            def fn_h(t, tp=tp, fc=fc):
                last = None
                for dc in range(8):
                    last = t.matmul(psH[:, 0:256], w1_sb[:, dc, fc * 128:(fc + 1) * 128],
                                    u_fm[:, dc, tp * 256:(tp + 1) * 256],
                                    start=(dc == 0), stop=(dc == 7))
                return last
            e.op(TE, fn_h, reads=("w1", "ufm_u"), writes=("psH",))
            e.op(SC, lambda s, fc=fc: s.activation(out=h_sb[:], in_=psH[:, 0:256],
                                                   func=AF.Gelu,
                                                   bias=b1c[:, fc:fc + 1], scale=1.0),
                 reads=("psH", "b1c"), writes=("hsb",))

            def fn_r(t, tp=tp, fc=fc):
                last = []
                for ti in range(2):
                    for ng in range(2):
                        last.append(t.matmul(
                            psR2[:, ti * 1024 + ng * 512:ti * 1024 + (ng + 1) * 512],
                            h_sb[:, ti * 128:(ti + 1) * 128],
                            w2_sb[:, fc, ng * 512:(ng + 1) * 512],
                            start=(fc == 0), stop=False))
                return last[-1]
            e.op(TE, fn_r, reads=("hsb", "w2"), writes=("psR2",))
        def fn_rb(t, tp=tp):
            last = []
            for ti in range(2):
                for ng in range(2):
                    last.append(t.matmul(
                        psR2[:, ti * 1024 + ng * 512:ti * 1024 + (ng + 1) * 512],
                        ones_row_bf[:], b2r[0:1, ng * 512:(ng + 1) * 512],
                        start=False, stop=True))
            return last[-1]
        e.op(TE, fn_rb, reads=("onesrowbf", "b2r"), writes=("psR2",))

        for ti in range(2):
            e.op(VE, lambda v, ti=ti: v.tensor_copy(out=r_sb[:],
                                                    in_=psR2[:, ti * 1024:(ti + 1) * 1024]),
                 reads=("psR2",), writes=("rsb",))

            def fn_tr(t):
                last = None
                for dc in range(8):
                    last = t.transpose(psT[:, dc * 128:(dc + 1) * 128],
                                       r_sb[:, dc * 128:(dc + 1) * 128], ident_bf[:])
                return last
            e.op(TE, fn_tr, reads=("rsb", "identbf"), writes=("psT",))
            e.op(VE, lambda v: v.tensor_copy(
                out=r_fm[:], in_=psT[:].rearrange("p (a b) -> p a b", b=128)),
                reads=("psT",), writes=("rfm",))

            i = tp * 2 + ti
            e.op(SY, lambda s, i=i: s.dma_start(out=xt[i % 2][:],
                                                in_=x1_d[i * 128:(i + 1) * 128, :]),
                 writes=(f"xt{i % 2}",), dma=True)
            for ng in range(2):
                def fn_o(t, ng=ng):
                    last = None
                    for pc in range(8):
                        last = t.matmul(psH[:, 0:512], r_fm[:, pc, :],
                                        wo_sb[:, pc, ng * 512:(ng + 1) * 512],
                                        start=(pc == 0), stop=False)
                    return t.matmul(psH[:, 0:512], ones_row_bf[:],
                                    bor[0:1, ng * 512:(ng + 1) * 512],
                                    start=False, stop=True)
                e.op(TE, fn_o, reads=("rfm", "wosb", "onesrowbf", "bor"), writes=("psH",))
                e.op(VE, lambda v, i=i, ng=ng: v.tensor_scalar(
                    out=u1t[:, ng * 512:(ng + 1) * 512],
                    in0=xt[i % 2][:, ng * 512:(ng + 1) * 512],
                    scalar1=lnstat[:, 2 * i:2 * i + 1], scalar2=lnstat[:, 2 * i + 1:2 * i + 2],
                    op0=ALU.subtract, op1=ALU.mult),
                    reads=(f"xt{i % 2}", "lnstat_u"), writes=("u1t",))
                e.op(VE, lambda v, i=i, ng=ng: v.tensor_add(
                    out=u1t[:, ng * 512:(ng + 1) * 512],
                    in0=u1t[:, ng * 512:(ng + 1) * 512],
                    in1=xt[i % 2][:, ng * 512:(ng + 1) * 512]),
                    reads=(f"xt{i % 2}", "u1t"), writes=("u1t",))
                e.op(VE, lambda v, ng=ng: v.scalar_tensor_tensor(
                    out=x2t[:, ng * 512:(ng + 1) * 512], in0=psH[:, 0:512], scalar=0.3,
                    in1=u1t[:, ng * 512:(ng + 1) * 512], op0=ALU.mult, op1=ALU.add),
                    reads=("psH", "u1t"), writes=("x2t",))
            e.op(SY, lambda s, i=i: s.dma_start(out=x2_d[i * 128:(i + 1) * 128, :],
                                                in_=x2t[:]),
                 reads=("x2t",), writes=(f"x2o{i}",), dma=True, dkey="x2out")

    e.op(VE, lambda v: v.engine_nop(), reads=(), writes=("ufm_u",))
    emit_ln_tiles(e, p, x2_d, u2_fm,
                  dict(xt=xt, stat=stat, ubf=ubf, psT=psT, ident_bf=ident_bf), ukey="u2",
                  src_key=[f"x2o{i}" for i in range(16)])

    e.op(SY, lambda s: s.dma_start(out=u2fm_d[:], in_=u2_fm[:]), reads=("ufm_u2",),
         dma=True, dkey="u2out")
    e.flush()
    return p


def build_l1u():
    p = Prog()
    nc = p.nc
    e = Emit(p)
    u2f_d = nc.dram_tensor("u2_full", [4, 128, 8, NT], BF16, kind="ExternalInput").ap()
    wq_d = nc.dram_tensor("WqT", [D, 256], F32, kind="ExternalInput").ap()
    wk_d = nc.dram_tensor("WkT", [D, 256], F32, kind="ExternalInput").ap()
    wv_d = nc.dram_tensor("WvT", [D, 256], F32, kind="ExternalInput").ap()
    o_d = nc.dram_tensor("O", [128, 2, 4 * NT], BF16, kind="ExternalOutput").ap()

    wq_sb = p.sbuf("wqsb", [128, 8, 256], BF16)
    wk_sb = p.sbuf("wksb", [128, 8, 256], BF16)
    wv_sb = p.sbuf("wvsb", [128, 8, 256], BF16)
    u2r = [p.sbuf(f"u2r{i}", [128, 8, NT], BF16) for i in range(2)]
    q_fm = p.sbuf("qfm", [128, 2, 4 * NT], BF16)
    k_fm = p.sbuf("kfm", [128, 2, 4 * NT], BF16)
    v_tm = p.sbuf("vtm", [128, 64, 260], BF16)
    pt_sb = p.sbuf("ptsb", [128, 2048], BF16)
    sq_sc = p.sbuf("sqsc", [128, 2048], BF16)
    bmax = p.sbuf("bmax", [1, 64], F32)
    negb2 = p.sbuf("negb2", [128, 2], F32)
    linv = p.sbuf("linv", [1, 512], F32)
    linvb = p.sbuf("linvb", [64, 512], F32)
    ot = [p.sbuf(f"ot{i}", [128, 512], BF16) for i in range(2)]
    ones_row = p.sbuf("onesrow", [1, 128], F32)
    ones_col = p.sbuf("onescol", [128, 1], BF16)

    psS = p.psum("psS", [128, 2048], F32)
    psA = [p.psum("psA0", [128, 512], F32), p.psum("psA1", [128, 512], F32)]
    psB = p.psum("psB", [128, 512], F32)

    SY, TE, VE, SC, GP = ENGINES

    make_const(e, GP, ones_row[:], "onesrow", "ones")
    make_const(e, GP, ones_col[:], "onescol", "ones")
    e.op(GP, lambda g: g.memset(
        v_tm[:].rearrange("p t (h x) -> p t h x", x=65)[:, :, :, 64:65], 1.0),
        writes=("vtm_ones",))
    e.op(GP, lambda g: g.memset(bmax[:], 0.0),
         writes=("bmax_q", "bmax_k", "bmax_t", "bmax_s"))

    for w_d, w_sb, tag in ((wq_d, wq_sb, "wq"), (wk_d, wk_sb, "wk"),
                           (wv_d, wv_sb, "wv")):
        e.op(GP, lambda g, w_d=w_d, w_sb=w_sb: g.dma_start(
            out=w_sb[:], in_=w_d.rearrange("(a p) f -> p a f", p=128)),
            writes=(tag,), dma=True, dkey=tag)

    for r in range(4):
        ub, ukk = u2r[r % 2], f"u2r{r % 2}"
        e.op(SY, lambda s, r=r, ub=ub: s.dma_start(out=ub[:], in_=u2f_d[r]),
             writes=(ukk,), dma=True, dkey=ukk)
        for dst, dk, tag in ((q_fm, "qfm", "wq"), (k_fm, "kfm", "wk")):
            w_sb = {"wq": wq_sb, "wk": wk_sb}[tag]
            for oc in range(2):
                for tg in range(4):
                    def fn_mm(t, oc=oc, tg=tg, ub=ub, w_sb=w_sb):
                        last = None
                        for dc in range(8):
                            last = t.matmul(psB[:], w_sb[:, dc, oc * 128:(oc + 1) * 128],
                                            ub[:, dc, tg * 512:(tg + 1) * 512],
                                            start=(dc == 0), stop=(dc == 7))
                        return last
                    e.op(TE, fn_mm, reads=(tag, ukk), writes=("psB",))
                    e.op(VE, lambda v, oc=oc, tg=tg, r=r, dst=dst, dk=dk: v.tensor_copy(
                        out=dst[:, oc, r * NT + tg * 512:r * NT + (tg + 1) * 512],
                        in_=psB[:]),
                        reads=("psB",), writes=(dk,))
        for ti in range(16):
            def fn_vm(t, ti=ti, ub=ub):
                last = None
                for dc in range(8):
                    last = t.matmul(psB[:, 0:256], ub[:, dc, ti * 128:(ti + 1) * 128],
                                    wv_sb[:, dc, :], start=(dc == 0), stop=(dc == 7))
                return last
            e.op(TE, fn_vm, reads=("wv", ukk), writes=("psB",))
            e.op(VE, lambda v, ti=ti, r=r: v.tensor_copy(
                out=v_tm[:, r * 16 + ti].rearrange("p (h x) -> p h x", x=65)[:, :, 0:64],
                in_=psB[:, 0:256].rearrange("p (h d) -> p h d", d=64)),
                reads=("psB",), writes=("vtm",))

    def emit_norms(src_ap, src_keys, out_off, tag):
        e.op(GP, lambda g: g.tensor_mul(sq_sc[:], src_ap, src_ap),
             reads=src_keys, writes=("sqsc",))

        def fn_nq(t):
            last = []
            for s in range(2):
                for g in range(4):
                    last.append(t.matmul(
                        psS[0:1, s * 1024 + (g % 2) * 512:s * 1024 + (g % 2) * 512 + 512],
                        ones_col[64 * s:64 * s + 64, :],
                        sq_sc[64 * s:64 * s + 64, g * 512:(g + 1) * 512],
                        start=(g < 2), stop=(g >= 2)))
            return last
        e.op(TE, fn_nq, reads=("sqsc", "onescol"), writes=("psS",))

        def fn_red(v):
            v.reduce_max(bmax[:, 30:31], psS[0:1, 0:1024], axis=AX)
            return v.reduce_max(bmax[:, 31:32], psS[0:1, 1024:2048], axis=AX)
        e.op(VE, fn_red, reads=("psS",), writes=("bmax_t",))
        e.op(VE, lambda v: v.tensor_max(bmax[:, out_off:out_off + 2],
                                        bmax[:, out_off:out_off + 2], bmax[:, 30:32]),
             reads=("bmax_t", tag), writes=(tag,))

    for hp in range(2):
        for r in range(4):
            emit_norms(q_fm[:, hp, r * NT:(r + 1) * NT], ("qfm",), 2 * hp, "bmax_q")
            emit_norms(k_fm[:, hp, r * NT:(r + 1) * NT], ("kfm",), 8 + 2 * hp, "bmax_k")

    for hp in range(2):
        e.op(VE, lambda v, hp=hp: v.tensor_mul(bmax[:, 24:26], bmax[:, 2 * hp:2 * hp + 2],
                                               bmax[:, 8 + 2 * hp:8 + 2 * hp + 2]),
             reads=("bmax_q", "bmax_k"), writes=("bmax_s",))
        e.op(VE, lambda v: v.tensor_max(bmax[:, 26:27], bmax[:, 24:25], bmax[:, 25:26]),
             reads=("bmax_s",), writes=("bmax_s",))
        e.op(SC, lambda s: s.activation(out=bmax[:, 27:28], in_=bmax[:, 26:27],
                                        func=AF.Sqrt), reads=("bmax_s",), writes=("bmax_s",))
        e.op(VE, lambda v: v.tensor_scalar(out=bmax[:, 28:29], in0=bmax[:, 27:28],
                                           scalar1=-0.1275, scalar2=None, op0=ALU.mult),
             reads=("bmax_s",), writes=("bmax_s",))
        e.op(TE, lambda t: t.matmul(psB[:, 0:1], ones_row[:], bmax[0:1, 28:29],
                                    start=True, stop=True),
             reads=("bmax_s", "onesrow"), writes=("psB",))
        e.op(SC, lambda s, hp=hp: s.copy(out=negb2[:, hp:hp + 1], in_=psB[:, 0:1]),
             reads=("psB",), writes=("negb",))

    for hp in range(2):
        for qt in range(16):
            ntt = 4 * (qt + 1)
            first_pv = True
            for tb0 in range(0, ntt, 2):
                nb = min(2, ntt - tb0)

                def fn_s(t, hp=hp, qt=qt, tb0=tb0, nb=nb):
                    last = []
                    for tb in range(nb):
                        tt = tb0 + tb
                        for s in range(2):
                            last.append(t.matmul(
                                psS[:, (tb * 2 + s) * 512:(tb * 2 + s + 1) * 512],
                                k_fm[64 * s:64 * s + 64, hp, tt * 128:(tt + 1) * 128],
                                q_fm[64 * s:64 * s + 64, hp, qt * 512:(qt + 1) * 512],
                                start=True, stop=True))
                    return last
                e.op(TE, fn_s, reads=("kfm", "qfm"), writes=("psS",))
                e.op(SC, lambda s, nb=nb, hp=hp: s.activation(
                    out=pt_sb[:, 0:nb * 1024], in_=psS[:, 0:nb * 1024], func=AF.Exp,
                    bias=negb2[:, hp:hp + 1], scale=0.125),
                    reads=("psS", "negb"), writes=("ptsb",))
                mops = [(tb0 + tb, tb, s) for tb in range(nb) for s in range(2)
                        if 128 * (tb0 + tb + 1) > 512 * qt]
                if mops:
                    def fn_mask(g, mops=mops, qt=qt):
                        last = None
                        for tt, tb, s in mops:
                            base = 512 * qt - 128 * tt
                            last = g.affine_select(
                                out=pt_sb[:, (tb * 2 + s) * 512:(tb * 2 + s + 1) * 512],
                                in_=pt_sb[:, (tb * 2 + s) * 512:(tb * 2 + s + 1) * 512],
                                compare_op=ALU.is_ge, fill=0.0, base=base,
                                pattern=[[1, 512]], channel_multiplier=-1)
                        return last
                    e.op(GP, fn_mask, reads=("ptsb",), writes=("ptsb",))

                def fn_pv(t, hp=hp, tb0=tb0, nb=nb, first=first_pv,
                          last_batch=(tb0 + 2 >= ntt)):
                    out = []
                    for tb in range(nb):
                        tt = tb0 + tb
                        for s in range(2):
                            lh = 2 * hp + s
                            out.append(t.matmul(
                                psA[s][0:65, :],
                                v_tm[:, tt, lh * 65:(lh + 1) * 65],
                                pt_sb[:, (tb * 2 + s) * 512:(tb * 2 + s + 1) * 512],
                                start=(first and tb == 0),
                                stop=(last_batch and tb == nb - 1),
                                skip_group_check=True))
                    return out
                e.op(TE, fn_pv, reads=("vtm", "vtm_ones", "ptsb"),
                     writes=("psA0", "psA1"))
                first_pv = False

            ob, obk = ot[qt % 2], f"ot{qt % 2}"
            for s in range(2):
                e.op(VE, lambda v, s=s: v.tensor_scalar(
                    out=linv[:], in0=psA[s][64:65, :], scalar1=1e-35, scalar2=None,
                    op0=ALU.max), reads=(f"psA{s}",), writes=("linv",))
                e.op(VE, lambda v: v.reciprocal(out=linv[:], in_=linv[:]),
                     reads=("linv",), writes=("linv",))
                e.op(TE, lambda t: t.matmul(psB[0:64, :], ones_row[0:1, 0:64], linv[:],
                                            start=True, stop=True),
                     reads=("linv", "onesrow"), writes=("psB",))
                e.op(SC, lambda s_: s_.copy(out=linvb[:], in_=psB[0:64, :]),
                     reads=("psB",), writes=("linvb",))
                e.op(VE, lambda v, s=s, ob=ob, obk=obk: v.tensor_tensor(
                    out=ob[64 * s:64 * s + 64, :],
                    in0=psA[s][0:64, :], in1=linvb[:], op=ALU.mult),
                    reads=(f"psA{s}", "linvb"), writes=(obk,))
            e.op(SY, lambda s, hp=hp, qt=qt, ob=ob, obk=obk: s.dma_start(
                out=o_d[:, hp, qt * 512:(qt + 1) * 512], in_=ob[:]),
                reads=(obk,), dma=True, dkey="oout")

    e.flush()
    return p


def build_l1f():
    p = Prog()
    nc = p.nc
    e = Emit(p)
    x2_d = nc.dram_tensor("x2", [NT, D], F32, kind="ExternalInput").ap()
    o_d = nc.dram_tensor("O", [128, 8, NT], BF16, kind="ExternalInput").ap()
    wot_d = nc.dram_tensor("WoT", [D, D], F32, kind="ExternalInput").ap()
    gw_d = nc.dram_tensor("gate_w", [8, D], F32, kind="ExternalInput").ap()
    w1_d = nc.dram_tensor("w1", [8, D, 2048], F32, kind="ExternalInput").ap()
    w2_d = nc.dram_tensor("w2", [8, 2048, D], F32, kind="ExternalInput").ap()
    w3_d = nc.dram_tensor("w3", [8, D, 2048], F32, kind="ExternalInput").ap()
    sw1_d = nc.dram_tensor("sw1", [1, D, 2048], F32, kind="ExternalInput").ap()
    sw2_d = nc.dram_tensor("sw2", [1, 2048, D], F32, kind="ExternalInput").ap()
    sw3_d = nc.dram_tensor("sw3", [1, D, 2048], F32, kind="ExternalInput").ap()
    x4_d = nc.dram_tensor("x4", [NT, D], F32, kind="ExternalOutput").ap()
    x3_d = nc.dram_tensor("x3s", [NT, D], F32, kind="Internal").ap()

    xt = [p.sbuf(f"xt{i}", [128, D], F32) for i in range(2)]
    stat = [p.sbuf(f"stat{i}", [128, 16], F32) for i in range(2)]
    ubf = [p.sbuf(f"ubf{i}", [128, D], BF16) for i in range(2)]
    u_fm = p.sbuf("ufm", [128, 8, NT], BF16)
    o_sb = u_fm
    y_acc = p.sbuf("yacc", [128, 16, D], F32)
    w1h = [p.sbuf(f"w1h{i}", [128, 8, D], BF16) for i in range(2)]
    w3h = [p.sbuf(f"w3h{i}", [128, 8, D], BF16) for i in range(2)]
    w2h_t = p.sbuf("w2h", [128, 8, D], BF16)
    w2h = [w2h_t, w2h_t]
    act_buf = p.sbuf("actbuf", [128, 8, 512], BF16)
    s1_sb = p.sbuf("s1sb", [128, 512], BF16)
    gwt_sb = p.sbuf("gwtsb", [128, 8, 8], BF16)
    comb = p.sbuf("comb", [128, 16, 9], F32)
    gz = p.sbuf("gz", [128, 8], F32)
    gm = p.sbuf("gm", [128, 8], F32)
    negm1 = p.sbuf("negm1", [128, 1], F32)
    gsc = p.sbuf("gsc", [128, 8], F32)
    ident_bf = p.sbuf("identbf", [128, 128], BF16)
    x3t = p.sbuf("x3t", [128, D], F32)
    x4t = p.sbuf("x4t", [128, D], F32)
    wot_sb = w2h_t

    psH1 = p.psum("psH1", [128, 512], F32)
    psH3 = p.psum("psH3", [128, 512], F32)
    psY = [p.psum("psY0", [128, 512], F32), p.psum("psY1", [128, 512], F32)]
    psT = p.psum("psT", [128, 1024], BF16)
    psZ = p.psum("psZ", [128, 512], F32)
    gw_sb = act_buf[:].rearrange("p a b -> p (a b)")[0:8, 0:1024]

    SY, TE, VE, SC, GP = ENGINES
    make_const(e, GP, ident_bf[:], "identbf", "ident")
    e.op(GP, lambda g: g.memset(y_acc[:], 0.0), writes=("yacc",))

    e.op(GP, lambda g: g.dma_start(out=o_sb[:], in_=o_d[:]),
         writes=("ufm_u",), dma=True, dkey="oload")
    e.op(GP, lambda g: g.dma_start(out=wot_sb[:],
                                   in_=wot_d[:].rearrange("(a p) b -> p a b", p=128)),
         writes=("wot",), dma=True, dkey="wotload")
    x3b = [x3t, x4t]
    for i in range(NTILE):
        e.op(SY, lambda s, i=i: s.dma_start(out=xt[i % 2][:],
                                            in_=x2_d[i * 128:(i + 1) * 128, :]),
             writes=(f"xt{i % 2}",), dma=True)
        xb = x3b[i % 2]
        xbk = ("x3t", "x4t")[i % 2]
        for ng in range(2):
            def fn_y(t, i=i, ng=ng):
                last = None
                for mc in range(8):
                    last = t.matmul(psY[0][:], o_sb[:, mc, i * 128:(i + 1) * 128],
                                    wot_sb[:, mc, ng * 512:(ng + 1) * 512],
                                    start=(mc == 0), stop=(mc == 7))
                return last
            e.op(TE, fn_y, reads=("ufm_u", "wot"), writes=("psY0",))
            e.op(VE, lambda v, i=i, ng=ng, xb=xb: v.tensor_add(
                out=xb[:, ng * 512:(ng + 1) * 512],
                in0=xt[i % 2][:, ng * 512:(ng + 1) * 512], in1=psY[0][:]),
                reads=(f"xt{i % 2}", "psY0"), writes=(xbk,))
        e.op(SY, lambda s, i=i, xb=xb: s.dma_start(
            out=x3_d[i * 128:(i + 1) * 128, :], in_=xb[:]),
             reads=(xbk,), writes=(f"x3o{i}",), dma=True, dkey=f"x3out{i % 2}")

    emit_ln_tiles(e, p, x3_d, u_fm,
                  dict(xt=xt, stat=stat, ubf=ubf, psT=psT, ident_bf=ident_bf), ukey="u",
                  src_key=[f"x3o{i}" for i in range(16)])

    e.op(GP, lambda g: g.dma_start(out=gw_sb, in_=gw_d[:]), writes=("gw",), dma=True)
    for dc in range(8):
        e.op(TE, lambda t, dc=dc: t.transpose(psZ[:, 0:8].bitcast(BF16)[:, 0:8],
                                              gw_sb[:, dc * 128:(dc + 1) * 128],
                                              ident_bf[0:8, 0:8]),
             reads=("gw", "identbf"), writes=("psZ",))
        e.op(VE, lambda v, dc=dc: v.tensor_copy(out=gwt_sb[:, dc, :],
                                                in_=psZ[:, 0:8].bitcast(BF16)[:, 0:8]),
             reads=("psZ",), writes=("gwt",))
    for ti in range(NTILE):
        def fn_z(t, ti=ti):
            last = None
            for dc in range(8):
                last = t.matmul(psZ[:, 16:24], u_fm[:, dc, ti * 128:(ti + 1) * 128],
                                gwt_sb[:, dc, :], start=(dc == 0), stop=(dc == 7))
            return last
        e.op(TE, fn_z, reads=("ufm_u", "gwt"), writes=("psZ2",))
        e.op(VE, lambda v: v.tensor_copy(out=gz[:], in_=psZ[:, 16:24]),
             reads=("psZ2",), writes=("gz",))
        e.op(VE, lambda v: v.max(out=gm[:], in_=gz[:]), reads=("gz",), writes=("gm",))
        e.op(VE, lambda v: v.tensor_scalar(out=negm1[:], in0=gm[:, 0:1], scalar1=-1.0,
                                           scalar2=None, op0=ALU.mult),
             reads=("gm",), writes=("negm1",))
        e.op(SC, lambda s: s.activation(out=gsc[:], in_=gz[:], func=AF.Exp,
                                        bias=negm1[:], scale=1.0),
             reads=("gz", "negm1"), writes=("gsc",))
        e.op(SC, lambda s: s.activation(out=negm1[:], in_=gm[:, 1:2], func=AF.Exp,
                                        bias=negm1[:], scale=1.0),
             reads=("gm", "negm1"), writes=("negm1b",))
        e.op(VE, lambda v: v.tensor_scalar_add(negm1[:], negm1[:], 1.0),
             reads=("negm1b",), writes=("negm1b",))
        e.op(VE, lambda v: v.reciprocal(out=negm1[:], in_=negm1[:]),
             reads=("negm1b",), writes=("negm1b",))
        e.op(VE, lambda v: v.tensor_scalar(out=gsc[:], in0=gsc[:], scalar1=negm1[:],
                                           scalar2=None, op0=ALU.mult),
             reads=("gsc", "negm1b"), writes=("gsc",))
        e.op(VE, lambda v: v.tensor_scalar(out=gm[:], in0=gz[:], scalar1=gm[:, 1:2],
                                           scalar2=None, op0=ALU.is_ge),
             reads=("gz", "gm"), writes=("gmask",))
        e.op(VE, lambda v, ti=ti: v.tensor_mul(comb[:, ti, 0:8], gsc[:], gm[:]),
             reads=("gsc", "gmask"), writes=("comb",))
        e.op(VE, lambda v, ti=ti: v.memset(comb[:, ti, 8:9], 1.0),
             reads=(), writes=("comb",))

    e.op(VE, lambda v: v.engine_nop(), reads=(), writes=("gw",))

    for eh in range(18):
        ex, half = eh // 2, eh % 2
        wb1, wb3, wb2 = w1h[eh % 2], w3h[eh % 2], w2h[eh % 2]
        wk1, wk3, wk2 = f"w1h{eh % 2}", f"w3h{eh % 2}", "w2h"
        if ex < 8:
            s1d = w1_d[ex][:, half * 1024:(half + 1) * 1024]
            s3d = w3_d[ex][:, half * 1024:(half + 1) * 1024]
            s2d = w2_d[ex][half * 1024:(half + 1) * 1024, :]
        else:
            s1d = sw1_d[0][:, half * 1024:(half + 1) * 1024]
            s3d = sw3_d[0][:, half * 1024:(half + 1) * 1024]
            s2d = sw2_d[0][half * 1024:(half + 1) * 1024, :]
        e.op(GP, lambda g, s1d=s1d, wb1=wb1: g.dma_start(
            out=wb1[:], in_=s1d.rearrange("(a p) f -> p a f", p=128)),
            writes=(wk1,), dma=True, dkey=wk1)
        e.op(GP, lambda g, s3d=s3d, wb3=wb3: g.dma_start(
            out=wb3[:], in_=s3d.rearrange("(a p) f -> p a f", p=128)),
            writes=(wk3,), dma=True, dkey=wk3)
        e.op(GP, lambda g, s2d=s2d, wb2=wb2: g.dma_start(
            out=wb2[:], in_=s2d.rearrange("(a p) f -> p a f", p=128)),
            writes=(wk2,) + (("wot",) if eh == 0 else ()), dma=True, dkey=wk2)

        for tq in range(4):
            for fc in range(8):
                def fn_h(t, tq=tq, fc=fc, wb1=wb1, wb3=wb3):
                    last = []
                    for dc in range(8):
                        t.matmul(psH1[:], wb1[:, dc, fc * 128:(fc + 1) * 128],
                                 u_fm[:, dc, tq * 512:(tq + 1) * 512],
                                 start=(dc == 0), stop=(dc == 7))
                    for dc in range(8):
                        last.append(t.matmul(psH3[:], wb3[:, dc, fc * 128:(fc + 1) * 128],
                                             u_fm[:, dc, tq * 512:(tq + 1) * 512],
                                             start=(dc == 0), stop=(dc == 7)))
                    return last[-1]
                e.op(TE, fn_h, reads=(wk1, wk3, "ufm_u"), writes=("psH1", "psH3"))
                e.op(SC, lambda s: s.activation(out=s1_sb[:], in_=psH1[:], func=AF.Silu),
                     reads=("psH1",), writes=("s1sb",))
                e.op(VE, lambda v, fc=fc: v.tensor_mul(act_buf[:, fc, :], s1_sb[:], psH3[:]),
                     reads=("s1sb", "psH3"), writes=("actbuf",))
            for ti in range(4):
                gi = tq * 4 + ti
                for ng in range(2):
                    def fn_y(t, ti=ti, ng=ng, wb2=wb2):
                        last = None
                        for fc in range(8):
                            last = t.matmul(psY[ng][:], act_buf[:, fc, ti * 128:(ti + 1) * 128],
                                            wb2[:, fc, ng * 512:(ng + 1) * 512],
                                            start=(fc == 0), stop=(fc == 7))
                        return last
                    e.op(TE, fn_y, reads=("actbuf", wk2), writes=(f"psY{ng}",))
                    e.op(VE, lambda v, gi=gi, ng=ng, ex=ex: v.scalar_tensor_tensor(
                        out=y_acc[:, gi, ng * 512:(ng + 1) * 512], in0=psY[ng][:],
                        scalar=comb[:, gi, ex:ex + 1],
                        in1=y_acc[:, gi, ng * 512:(ng + 1) * 512],
                        op0=ALU.mult, op1=ALU.add),
                        reads=(f"psY{ng}", "comb", "yacc"), writes=("yacc",))

    for i in range(NTILE):
        e.op(SY, lambda s, i=i: s.dma_start(out=x3t[:], in_=x3_d[i * 128:(i + 1) * 128, :]),
             reads=(f"x3o{i}",), writes=("x3t",), dma=True, dkey="x3ld")
        e.op(VE, lambda v, i=i: v.tensor_add(out=x4t[:], in0=x3t[:], in1=y_acc[:, i, :]),
             reads=("x3t", "yacc"), writes=("x4t",))
        e.op(SY, lambda s, i=i: s.dma_start(out=x4_d[i * 128:(i + 1) * 128, :], in_=x4t[:]),
             reads=("x4t",), dma=True, dkey="x4out")

    e.flush()
    return p


def shard_x(x_full):
    out = []
    for c in range(NCORES):
        b, j = c // 4, c % 4
        out.append(np.ascontiguousarray(x_full[b, j * NT:(j + 1) * NT, :]))
    return out


_PROG_CACHE = {}


def _prog(name, builder):
    if name not in _PROG_CACHE:
        _PROG_CACHE[name] = builder()
    return _PROG_CACHE[name]


def kernel(**inputs):
    f32 = lambda a: np.ascontiguousarray(a, dtype=np.float32)
    x = f32(inputs["L0_attn_x"])
    shards = shard_x(x)

    p1 = _prog("l0a", build_l0a)
    w_l0a = {k: f32(inputs["L0_attn_" + k]) for k in ("Wq", "Wk", "Wv", "Wa", "ba")}
    maps1 = [dict(x=shards[c], **w_l0a) for c in range(NCORES)]
    r1 = run_bass_kernel_spmd(p1.nc, maps1, core_ids=list(range(NCORES))).results

    p2 = _prog("l0b", build_l0b)
    maps2 = []
    for c in range(NCORES):
        b, jp = c // 4, c % 4
        S_prev = np.zeros((3, 128, 520), np.float32)
        A_prev = np.ones((3, H), np.float32)
        for k in range(jp):
            pc = b * 4 + k
            S_prev[3 - jp + k] = r1[pc]["S_end"]
            A_prev[3 - jp + k] = r1[pc]["A_core"][0]
        maps2.append(dict(x=shards[c], q_fm=r1[c]["q_fm"], k_fm=r1[c]["k_fm"],
                          v_tm=r1[c]["v_tm"], lna_tm=r1[c]["lna_tm"],
                          S_prev=S_prev, A_prev=A_prev, chunkS=r1[c]["chunkS"],
                          chunkA=r1[c]["chunkA"], Wo=f32(inputs["L0_attn_Wo"])))
    r2 = run_bass_kernel_spmd(p2.nc, maps2, core_ids=list(range(NCORES))).results

    p3 = _prog("l0f", build_l0f)
    w_l0f = {k: f32(inputs["L0_ffn_" + k]) for k in
             ("Wp1", "bp1", "Wp2", "bp2", "W1", "b1", "W2", "b2", "Wo", "bo")}
    maps3 = []
    for c in range(NCORES):
        b = c // 4
        lastvec = r2[b * 4 + 3]["u1last"]
        maps3.append(dict(x1=r2[c]["x1"], lastvec=lastvec, **w_l0f))
    r3 = run_bass_kernel_spmd(p3.nc, maps3, core_ids=list(range(NCORES))).results

    p4 = _prog("l1u", build_l1u)

    def wslice(w, g):
        return np.ascontiguousarray(w[256 * g:256 * (g + 1), :].T)
    maps4 = []
    for c in range(NCORES):
        b, g = c // 4, c % 4
        u2_full = np.stack([r3[b * 4 + k]["u2_fm"] for k in range(4)])
        maps4.append(dict(u2_full=u2_full,
                          WqT=wslice(f32(inputs["L1_attn_Wq"]), g),
                          WkT=wslice(f32(inputs["L1_attn_Wk"]), g),
                          WvT=wslice(f32(inputs["L1_attn_Wv"]), g)))
    r4 = run_bass_kernel_spmd(p4.nc, maps4, core_ids=list(range(NCORES))).results

    p5 = _prog("l1f", build_l1f)
    w_l1f = {k: f32(inputs["L1_ffn_" + k]) for k in
             ("gate_w", "w1", "w2", "w3", "sw1", "sw2", "sw3")}
    wot_full = np.ascontiguousarray(f32(inputs["L1_attn_Wo"]).T)
    maps5 = []
    for c in range(NCORES):
        b, jp = c // 4, c % 4
        o_all = np.empty((128, 8, NT), r4[0]["O"].dtype)
        for g in range(4):
            src = r4[b * 4 + g]["O"]
            for ch in range(2):
                o_all[:, 2 * g + ch, :] = src[:, ch, jp * NT:(jp + 1) * NT]
        maps5.append(dict(x2=r3[c]["x2"], O=o_all, WoT=wot_full, **w_l1f))
    r5 = run_bass_kernel_spmd(p5.nc, maps5, core_ids=list(range(NCORES))).results

    out = np.empty((2, 8192, D), np.float32)
    for c in range(NCORES):
        b, jp = c // 4, c % 4
        out[b, jp * NT:(jp + 1) * NT, :] = r5[c]["x4"]
    return out
